# version 06
# baseline (speedup 1.0000x reference)
import numpy as np
import ml_dtypes
from contextlib import ExitStack

import concourse.bass as bass
import concourse.mybir as mybir
from concourse.bass_utils import run_bass_kernel_spmd
from concourse.bass_core_rust import IndirectOffsetOnAxis

F32 = mybir.dt.float32
BF16 = mybir.dt.bfloat16
I32 = mybir.dt.int32
U32 = mybir.dt.uint32
AF = mybir.ActivationFunctionType
ALU = mybir.AluOpType
AX = mybir.AxisListType

S, H, FH = 16384, 1024, 4096
NCH, CHT, TPC = 32, 512, 64
NTOK = NCH * TPC
LR = 1e-3
EPS = 1e-5
RG = [[0, 1, 2, 3, 4, 5, 6, 7]]
SLAB = 3072
ZROW = (33 * 8) * 3
C_LOSS = LR / (CHT * H)
WCNT = float(H * FH)
NW = 16


def build_nc():
    nc = bass.Bass("TRN2", target_bir_lowering=False)

    xn_d = nc.dram_tensor("xn", [NTOK, H], F32, kind="ExternalInput").ap()
    xt_d = nc.dram_tensor("xt", [H, NTOK], F32, kind="ExternalInput").ap()
    w0_d = nc.dram_tensor("w0", [H, FH], F32, kind="ExternalInput").ap()
    w2_d = nc.dram_tensor("w2", [H, FH], F32, kind="ExternalInput").ap()
    w1_d = nc.dram_tensor("w1", [FH, H], F32, kind="ExternalInput").ap()
    fw1T_d = nc.dram_tensor("fw1T", [H, FH], F32, kind="ExternalInput").ap()
    fw2T_d = nc.dram_tensor("fw2T", [FH, H], F32, kind="ExternalInput").ap()
    cwv_d = nc.dram_tensor("cwv", [128, 96], F32, kind="ExternalInput").ap()
    cbv_d = nc.dram_tensor("cbv", [128, 32], F32, kind="ExternalInput").ap()
    idnb_d = nc.dram_tensor("idnb", [128, 128], BF16, kind="ExternalInput").ap()
    hidx_d = nc.dram_tensor("hidx", [2, NCH], U32, kind="ExternalInput").ap()
    o_d = nc.dram_tensor("o", [NTOK, H], F32, kind="ExternalOutput").ap()

    uv_hbm = nc.dram_tensor("uv_hbm", [2, NCH, 128, NTOK], F32, kind="Internal").ap()
    y_hbm = nc.dram_tensor("y_hbm", [NCH, TPC, H], F32, kind="Internal").ap()
    w2q_hbm = nc.dram_tensor("w2q_hbm", [128, 32768], BF16, kind="Internal").ap()
    ag_in = nc.dram_tensor("ag_in", [33, SLAB], F32, kind="Internal").ap()
    ag_all = nc.dram_tensor("ag_all", [33 * 8 + 1, SLAB], F32, kind="Internal",
                            addr_space="Shared").ap()
    ag_rows = ag_all.rearrange("a (s b) -> (a s) b", s=3)

    ctx = ExitStack()
    with ctx:
        e = ctx.enter_context
        SB_BYTES = 212736
        arena_t = e(nc.sbuf_tensor("arena", [128, SB_BYTES], mybir.dt.uint8))
        hmap = {F32: arena_t.bitcast(F32), BF16: arena_t.bitcast(BF16),
                I32: arena_t.bitcast(I32), U32: arena_t.bitcast(U32)}
        esz = {F32: 4, BF16: 2, I32: 4, U32: 4}
        cur = [0]

        def take(dt, p, n):
            off = (cur[0] + 63) // 64 * 64
            cur[0] = off + n * esz[dt]
            assert cur[0] <= SB_BYTES, f"arena overflow {cur[0]}"
            return hmap[dt][0:p, off // esz[dt]: off // esz[dt] + n]

        def at(dt, off_b, p, n):
            assert off_b % 64 == 0
            return hmap[dt][0:p, off_b // esz[dt]: off_b // esz[dt] + n]

        ones_col = take(F32, 128, 8)
        E128 = take(F32, 128, 128)
        ones128 = take(F32, 128, 128)
        idn_b = take(BF16, 128, 128)
        cw_t = take(F32, 128, 96)
        cb_t = take(F32, 128, 32)
        dbuf = [take(F32, 128, 8) for _ in range(2)]
        d3buf = [take(F32, 128, 8) for _ in range(2)]
        accw = [[take(F32, 128, 8) for _ in range(2)] for _ in range(2)]
        mwc = [take(F32, 128, 8) for _ in range(2)]
        swq = [take(F32, 128, 8) for _ in range(2)]
        t8a = take(F32, 128, 8)
        t8b = take(F32, 128, 8)
        lsb = [take(F32, 1, 8) for _ in range(2)]
        lrp = take(F32, 128, 8)
        pad8 = take(F32, 128, 8)
        sidx = take(U32, 2, NCH)
        PM = (cur[0] + 63) // 64 * 64

        cur[0] = PM
        w0bf = take(BF16, 128, 8 * FH)
        w2bf = take(BF16, 128, 8 * FH)
        xtbf = take(BF16, 128, 8 * NTOK)
        uvtmp = [take(F32, 128, 512) for _ in range(2)]
        fwbuf = take(F32, 128, FH)
        assert cur[0] <= SB_BYTES, f"P0 {cur[0]}"

        REGA = PM
        W1QT = PM + 65536
        POST = PM + 131072
        w1bf = at(BF16, REGA, 128, 32 * 1024)
        w2qT = at(BF16, REGA, 128, 32 * 1024)
        w1qT = at(BF16, W1QT, 128, 8 * FH)

        cur[0] = POST
        fsb = take(F32, 128, FH)
        ftmp = take(F32, 128, FH)
        itmp = take(I32, 128, FH)
        FSB_OFF = (POST + 63) // 64 * 64
        FT_OFF = (FSB_OFF + FH * 4 + 63) // 64 * 64
        bfv = at(BF16, FT_OFF, 128, FH)
        assert cur[0] <= SB_BYTES, f"P0b {cur[0]}"

        cur[0] = POST
        ubuf = [take(F32, 128, NTOK) for _ in range(2)]
        vbuf = [take(F32, 128, NTOK) for _ in range(2)]
        tsig = take(F32, 128, 1024)
        tt2 = take(F32, 128, 1024)
        gbf = take(BF16, 128, NTOK)
        xnb = [take(F32, 64, H) for _ in range(2)]
        g1t = take(F32, 64, H)
        tdf = take(F32, 64, H)
        trs = take(F32, 64, H)
        ybuf = [take(F32, 64, H) for _ in range(2)]
        lnsml = take(F32, 128, 16)
        assert cur[0] <= SB_BYTES, f"P1 {cur[0]}"

        cur[0] = POST
        yt = [take(F32, 66, H) for _ in range(2)]
        yqi = take(I32, 66, 512)
        yqb = take(BF16, 66, H)
        yqT = take(BF16, 128, 8 * 264)
        yAB = [take(F32, 128, H) for _ in range(2)]
        hsb = take(BF16, 128, 32 * 256)
        hqT = take(BF16, 128, 32 * 256)
        htmp = [take(F32, 128, 264) for _ in range(2)]
        cvA = take(F32, 128, 264)
        cvB = take(F32, 128, 264)
        cvC = take(F32, 128, 264)
        sinv_bc = take(F32, 128, 264)
        sq_bc = take(F32, 128, 256)
        hint = take(I32, 128, 256)
        rmax = take(F32, 128, 256)
        rmin = take(F32, 128, 256)
        rmaxb = take(BF16, 128, 256)
        ayt = take(F32, 66, 8)
        aycb = take(BF16, 66, 8)
        ayf = take(F32, 66, 8)
        sy66 = take(F32, 66, 8)
        t66a = take(F32, 66, 8)
        t66b = take(F32, 66, 8)
        mvp_s = take(F32, 128, 264)
        mvp_q = take(F32, 128, 256)
        amaxh = take(F32, 128, 8)
        shb16 = take(BF16, 128, 8)
        shbf = take(F32, 128, 8)
        shv = take(F32, 128, 8)
        cw_s = take(F32, 128, 96)
        outt = take(F32, 128, H)
        lnsml2 = take(F32, 128, 16)
        assert cur[0] <= SB_BYTES, f"P2 {cur[0]}"
        qtmp = cvA[:, 0:256]

        pA = e(nc.psum_tensor("pA", [128, 512], F32))
        pB = e(nc.psum_tensor("pB", [128, 512], F32))
        pC = e(nc.psum_tensor("pC", [128, 512], F32))
        pD = e(nc.psum_tensor("pD", [128, 512], F32))
        pE = e(nc.psum_tensor("pE", [128, 512], F32))
        pF = e(nc.psum_tensor("pF", [128, 512], F32))
        pS = e(nc.psum_tensor("pS", [128, 512], F32))
        pT = e(nc.psum_tensor("pT", [128, 1024], BF16))
        TPB = [pT[:, 0:1024], pS.bitcast(BF16)[:, 0:1024]]

        sems = {}

        def sem(name):
            if name not in sems:
                sems[name] = e(nc.semaphore(name))
            return sems[name]

        ph = sem("ph")

        def bar(eng, k, drain=True):
            if drain:
                eng.drain()
            eng.nop().then_inc(ph, 1)
            eng.wait_ge(ph, 5 * k)

        def gap(vec):
            vec.tensor_tensor(pad8[:, 0:1], pad8[:, 0:1], ones_col[:, 0:1], ALU.add)

        def recip1(vec, out8, in1ap, tmp8, p=128):
            vec.tensor_scalar_mul(tmp8[0:p, 0:8], ones_col[0:p, 0:8], in1ap)
            gap(vec)
            return vec.reciprocal(out8[0:p, 0:8], tmp8[0:p, 0:8])

        def fw_pair(t, buf):
            if t < 8:
                return buf[:, :], fw1T_d[t * 128:(t + 1) * 128, :]
            k4 = t - 8
            return (buf.rearrange("p (a b) -> p a b", a=4),
                    fw2T_d[k4 * 512:(k4 + 1) * 512, :].rearrange("(a p) b -> p a b", p=128))

        block = e(nc.Block())

        @block.sync
        def _(sy):
            sy.dma_start(out=cw_t[:, :], in_=cwv_d[:, :]).then_inc(sem("ld0"), 16)
            sy.dma_start(out=cb_t[:, :], in_=cbv_d[:, :]).then_inc(sem("ld0"), 16)
            sy.dma_start(out=idn_b[:, :], in_=idnb_d[:, :]).then_inc(sem("ld0"), 16)
            sy.dma_start(out=sidx[:, :], in_=hidx_d[:, :]).then_inc(sem("ld0"), 16)
            for n in range(0, 256, 2):
                g, uvi, m = n // 64, (n // 32) % 2, n % 32
                sy.wait_ge(sem("ev0"), n + 1)
                src = uvtmp[0].rearrange("p (c t) -> p c t", c=8)
                dst = uv_hbm[uvi, 8 * g:8 * g + 8, :, m * 64:(m + 1) * 64].rearrange("c p t -> p c t")
                sy.dma_start(out=dst, in_=src).then_inc(sem("st0a"), 16)
            sy.wait_ge(sem("st0a"), 16 * 128)
            sy.wait_ge(sem("st0b"), 16 * 128)
            bar(sy, 1, drain=False)
            for t in range(8, 16):
                sy.wait_ge(sem("tcast"), 2 * (t + 1))
                sy.dma_start(out=w2q_hbm[:, (t - 8) * 4096:(t - 7) * 4096], in_=bfv[:, :]).then_inc(sem("w2st"), 16)
            sy.wait_ge(sem("w2st"), 16 * 8)
            bar(sy, 2, drain=False)
            def uvld(i, slot):
                sy.dma_start(out=ubuf[slot][:, :], in_=uv_hbm[0, i, :, :]).then_inc(sem("uvl"), 16)
                sy.dma_start(out=vbuf[slot][:, :], in_=uv_hbm[1, i, :, :]).then_inc(sem("uvl"), 16)
                sy.dma_start(out=xnb[slot][:, :], in_=xn_d[i * 64:(i + 1) * 64, :]).then_inc(sem("uvl"), 16)
            uvld(0, 0)
            for i in range(NCH):
                if i < NCH - 1:
                    sy.wait_ge(sem("rs"), i)
                    uvld(i + 1, (i + 1) % 2)
                sy.wait_ge(sem("yy"), i + 1)
                sy.dma_start(out=y_hbm[i, :, :], in_=ybuf[i % 2][:, :]).then_inc(sem("yst"), 16)
            sy.wait_ge(sem("yst"), 16 * NCH)
            bar(sy, 3, drain=False)
            for q in range(4):
                sy.dma_start(out=w2qT[:, q * 8192:(q + 1) * 8192], in_=w2q_hbm[:, q * 8192:(q + 1) * 8192]).then_inc(sem("w2l"), 16)
            sy.wait_ge(sem("z0v"), 1)
            sy.dma_start(out=ag_rows[ZROW:ZROW + 1, :], in_=yAB[0][0:1, :]).then_inc(sem("z0"), 16)
            for b in range(8):
                for j in range(4):
                    c = 4 * b + j
                    if c >= 2:
                        sy.wait_ge(sem("q2"), c - 1)
                    sy.dma_start(out=yt[c % 2][2:66, :], in_=y_hbm[c, :, :]).then_inc(sem("yl"), 16)
                for p2 in range(2):
                    gp_ = 2 * b + p2
                    cA = 4 * b + 2 * p2
                    if gp_ >= 2:
                        sy.wait_ge(sem("ost"), 16 * (gp_ - 1))
                    sy.dma_start(out=yAB[gp_ % 2][:, :], in_=y_hbm[cA:cA + 2, :, :].rearrange("a b c -> (a b) c")).then_inc(sem("yab"), 16)
                    sy.wait_ge(sem("ot"), gp_ + 1)
                    sy.dma_start(out=o_d[64 * cA:64 * cA + 128, :], in_=yAB[gp_ % 2][:, :]).then_inc(sem("ost"), 16)
            sy.wait_ge(sem("ost"), 16 * 16)
            bar(sy, 4, drain=False)

        @block.gpsimd
        def _(gp):
            for k in range(8):
                gp.dma_start(out=xtbf[:, k * NTOK:(k + 1) * NTOK], in_=xt_d[k * 128:(k + 1) * 128, :]).then_inc(sem("ldw"), 16)
            for k in range(8):
                gp.dma_start(out=w0bf[:, k * FH:(k + 1) * FH], in_=w0_d[k * 128:(k + 1) * 128, :]).then_inc(sem("ldw"), 16)
            for k in range(8):
                gp.dma_start(out=w2bf[:, k * FH:(k + 1) * FH], in_=w2_d[k * 128:(k + 1) * 128, :]).then_inc(sem("ldw"), 16)
            for t in range(NW):
                if t >= 1:
                    gp.wait_ge(sem("mn"), t)
                dget = fw_pair(t, fwbuf)
                gp.dma_start(out=dget[0], in_=dget[1]).then_inc(sem("fw0"), 16)
            bar(gp, 1)
            for k in range(32):
                gp.dma_start(out=w1bf[:, k * 1024:(k + 1) * 1024], in_=w1_d[k * 128:(k + 1) * 128, :]).then_inc(sem("w1l"), 16)
            for t in range(NW):
                if t >= 1:
                    gp.wait_ge(sem("tclip"), t)
                dget = fw_pair(t, fsb)
                gp.dma_start(out=dget[0], in_=dget[1]).then_inc(sem("fwb"), 16)
            bar(gp, 2)
            for i in range(NCH):
                if i >= 1:
                    gp.wait_ge(sem("yy"), i)
                    gp.dma_start(out=ag_in[i:i + 1, 0:1024], in_=ybuf[(i - 1) % 2][0:1, :]).then_inc(sem("mail"), 16)
                    gp.dma_start(out=ag_in[i:i + 1, 1024:2048], in_=ybuf[(i - 1) % 2][63:64, :]).then_inc(sem("mail"), 16)
                gp.wait_ge(sem("lse"), i + 1)
                gp.dma_start(out=ag_in[i:i + 1, 2048:2049], in_=lsb[i % 2][0:1, 0:1]).then_inc(sem("lms"), 16)
                gp.wait_ge(sem("lms"), 16 * (i + 1))
                if i >= 1:
                    gp.wait_ge(sem("mail"), 32 * i)
                gp.collective_compute(
                    "AllGather", ALU.bypass, RG,
                    ins=[ag_in[i:i + 1, :]],
                    outs=[ag_all[8 * i:8 * (i + 1), :].rearrange("(o a) b -> o (a b)", o=1)],
                ).then_inc(sem("agd"), 1)
                gp.wait_ge(sem("agd"), i + 1)
                if i >= 1:
                    gp.wait_ge(sem("bc"), i)
                gp.dma_start(out=lrp[0:1, 0:8], in_=ag_all[8 * i:8 * (i + 1), 2048:2049].rearrange("a b -> b a"),
                             allow_slow_non_contiguous=True).then_inc(sem("lrq"), 16)
            gp.wait_ge(sem("yy"), NCH)
            gp.dma_start(out=ag_in[32:33, 0:1024], in_=ybuf[31 % 2][0:1, :]).then_inc(sem("mail"), 16)
            gp.dma_start(out=ag_in[32:33, 1024:2048], in_=ybuf[31 % 2][63:64, :]).then_inc(sem("mail"), 16)
            gp.dma_start(out=ag_in[32:33, 2048:2049], in_=lsb[1][0:1, 0:1]).then_inc(sem("mail"), 16)
            gp.wait_ge(sem("mail"), 32 * NCH + 16)
            gp.collective_compute(
                "AllGather", ALU.bypass, RG,
                ins=[ag_in[32:33, :]],
                outs=[ag_all[8 * 32:8 * 33, :].rearrange("(o a) b -> o (a b)", o=1)],
            ).then_inc(sem("agd"), 1)
            bar(gp, 3)
            gp.wait_ge(sem("agd"), NCH + 1)
            gp.wait_ge(sem("z0"), 16)
            gp.wait_ge(sem("ld0"), 16 * 4)
            for c in range(NCH):
                if c >= 2:
                    gp.wait_ge(sem("q2"), c - 1)
                gp.indirect_dma_start(
                    out=yt[c % 2][0:2, :], out_offset=None,
                    in_=ag_rows, in_offset=IndirectOffsetOnAxis(sidx[0:2, c:c + 1], 0),
                ).then_inc(sem("hg"), 16)
            bar(gp, 4)

        @block.tensor
        def _(tn):
            tn.wait_ge(sem("ldw"), 16 * 24)
            n = 0
            for g in range(4):
                for uvi in range(2):
                    wsrc = w0bf if uvi == 0 else w2bf
                    for m in range(32):
                        if n >= 2:
                            tn.wait_ge(sem("ev0"), n - 1)
                        bank = pA if n % 2 == 0 else pB
                        for k in range(8):
                            mm = tn.matmul(bank[:, :], wsrc[:, k * FH + m * 128:k * FH + (m + 1) * 128],
                                           xtbf[:, k * NTOK + g * 512:k * NTOK + (g + 1) * 512],
                                           start=(k == 0), stop=(k == 7))
                        mm.then_inc(sem("mm0"), 1)
                        n += 1
            tn.wait_ge(sem("mn"), NW + 1)
            for wi in range(2):
                mm = tn.matmul(pS[:, wi:wi + 1], ones128[:, :], accw[wi][0][:, 0:1], start=True, stop=True)
            mm.then_inc(sem("sw1"), 1)
            bar(tn, 1)
            bar(tn, 2)
            tn.wait_ge(sem("w1l"), 16 * 32)
            for i in range(NCH):
                tn.wait_ge(sem("gh"), 2 * (i + 1))
                if i >= 1:
                    tn.wait_ge(sem("ge1"), i)
                for ng in range(2):
                    bank = pA if ng == 0 else pB
                    for k in range(32):
                        mm = tn.matmul(bank[0:64, :], gbf[:, k * 64:(k + 1) * 64],
                                       w1bf[:, k * 1024 + ng * 512:k * 1024 + (ng + 1) * 512],
                                       start=(k == 0), stop=(k == 31))
                mm.then_inc(sem("mm1"), 1)
                tn.wait_ge(sem("lq"), i + 1)
                if i >= 1:
                    tn.wait_ge(sem("lse"), i)
                tn.matmul(pS[0:1, 0:1], ones_col[:, 0:1], lnsml[:, 0:1], start=True, stop=True).then_inc(sem("tm"), 1)
                tn.wait_ge(sem("lrq"), 16 * (i + 1))
                tn.wait_ge(sem("d"), i + 1)
                tn.matmul(pC[:, 0:8], E128[:, :], lrp[:, 0:8], start=True, stop=True).then_inc(sem("bc"), 1)
            bar(tn, 3)
            tn.wait_ge(sem("ld0"), 16 * 4)
            tn.wait_ge(sem("w2l"), 16 * 4)
            TPN = [0]

            def tpose(p, cols, in_ap, idn_ap):
                T = TPN[0]
                if T >= 2:
                    tn.wait_ge(sem("yqe"), T - 1)
                tn.transpose(TPB[T % 2][0:p, cols[0]:cols[1]], in_ap, idn_ap).then_inc(sem("tpq"), 1)
                TPN[0] += 1

            for b in range(8):
                for j in range(4):
                    c = 4 * b + j
                    tn.wait_ge(sem("q2"), c + 1)
                    for k in range(8):
                        tpose(128, (0, 66), yqb[:, k * 128:(k + 1) * 128], idn_b[0:66, 0:66])
                    tn.wait_ge(sem("q1"), c + 1)
                    tpose(1, (0, 66), aycb[0:66, 0:1], idn_b[0:66, 0:66])
                tn.wait_ge(sem("yqe"), 40 * b + 36)
                if b >= 1:
                    tn.wait_ge(sem("sb"), b)
                tn.matmul(pA[:, 0:264], E128[:, :], mvp_s[:, 0:264], start=True, stop=True).then_inc(sem("sbT"), 1)
                if b >= 1:
                    tn.wait_ge(sem("shc"), b)
                for m in range(32):
                    if m >= 2:
                        tn.wait_ge(sem("hT"), 32 * b + m - 1)
                    bank = pE if m % 2 == 0 else pF
                    for k in range(8):
                        mm = tn.matmul(bank[:, 0:264], w1qT[:, k * FH + m * 128:k * FH + (m + 1) * 128],
                                       yqT[:, k * 264:(k + 1) * 264], start=(k == 0), stop=(k == 7))
                    mm.then_inc(sem("up"), 1)
                tn.wait_ge(sem("rmb"), b + 1)
                tn.wait_ge(sem("hT"), 32 * (b + 1))
                tpose(128, (66, 194), rmaxb[:, 0:128], idn_b[:, :])
                tpose(128, (66, 194), rmaxb[:, 128:256], idn_b[:, :])
                tn.wait_ge(sem("shc"), b + 1)
                tpose(1, (194, 322), shb16[:, 0:1], idn_b[:, :])
                tpose(1, (194, 322), shb16[:, 1:2], idn_b[:, :])
                tn.wait_ge(sem("yqe"), 40 * (b + 1))
                if b >= 1:
                    tn.wait_ge(sem("amQ"), b)
                tn.matmul(pB[:, 0:256], E128[:, :], mvp_q[:, 0:256], start=True, stop=True).then_inc(sem("sqT"), 1)
                tn.wait_ge(sem("qa2"), 32 * (b + 1))
                for p2 in range(2):
                    gp_ = 2 * b + p2
                    if gp_ >= 1:
                        tn.wait_ge(sem("de1"), gp_)
                    for ng in range(2):
                        bank = pC if ng == 0 else pD
                        for k in range(32):
                            mm = tn.matmul(bank[:, :], hqT[:, k * 256 + p2 * 128:k * 256 + (p2 + 1) * 128],
                                           w2qT[:, k * 1024 + ng * 512:k * 1024 + (ng + 1) * 512],
                                           start=(k == 0), stop=(k == 31))
                    mm.then_inc(sem("dn"), 1)
            bar(tn, 4)

        @block.vector
        def _(ve):
            ve.memset(ones_col[:, :], 1.0)
            ve.memset(E128[:, :], 0.0)
            ve.memset(E128[0:1, :], 1.0)
            ve.memset(ones128[:, :], 1.0)
            ve.memset(lrp[:, :], 0.0)
            for wi in range(2):
                for pp in range(2):
                    ve.memset(accw[wi][pp][:, :], 0.0)
            for t in range(NW):
                wi, tl = (0, t) if t < 8 else (1, t - 8)
                ve.wait_ge(sem("fw0"), 16 * (t + 1))
                ve.tensor_reduce(t8a[:, 2:3], fwbuf[:, :], AX.X, ALU.add, apply_absolute_value=True)
                gap(ve)
                ve.tensor_tensor(accw[wi][(tl + 1) % 2][:, 0:1], accw[wi][tl % 2][:, 0:1], t8a[:, 2:3], ALU.add).then_inc(sem("mn"), 2 if t == NW - 1 else 1)
            ve.wait_ge(sem("sw1"), 1)
            for wi in range(2):
                ve.tensor_scalar(t8b[:, 0:1], pS[:, wi:wi + 1], 1.0 / WCNT, None, ALU.mult)
                gap(ve)
                ve.tensor_scalar_max(mwc[wi][:, 0:1], t8b[:, 0:1], 1e-5)
                gap(ve)
                r = recip1(ve, swq[wi], mwc[wi][:, 0:1], t8a)
            r.then_inc(sem("swd"), 1)
            bar(ve, 1)
            for t in range(NW):
                ve.wait_ge(sem("fwb"), 16 * (t + 1))
                if t >= 1:
                    ve.wait_ge(sem("tcast"), 2 * t)
                if t >= 9:
                    ve.wait_ge(sem("w2st"), 16 * (t - 8))
                wi = 0 if t < 8 else 1
                ve.tensor_scalar(ftmp[:, :], fsb[:, :], swq[wi][:, 0:1], -1.0, ALU.mult, ALU.max)
                ve.tensor_scalar(ftmp[:, :], ftmp[:, :], 1.0, None, ALU.min).then_inc(sem("tclip"), 1)
            bar(ve, 2)
            ve.memset(dbuf[0][:, :], 1.0)
            ve.memset(lnsml[:, :], 0.0)
            ve.memset(d3buf[0][:, :], 1.0).then_inc(sem("d"), 1)
            for i in range(NCH):
                ve.wait_ge(sem("uvl"), 48 * (i + 1))
                for hf in range(2):
                    ve.wait_ge(sem("sigh"), 2 * i + hf + 1)
                    ve.tensor_tensor(tt2[:, :], ubuf[i % 2][:, hf * 1024:(hf + 1) * 1024],
                                     vbuf[i % 2][:, hf * 1024:(hf + 1) * 1024], ALU.mult)
                    ve.tensor_tensor(gbf[:, hf * 1024:(hf + 1) * 1024], tsig[:, :], tt2[:, :], ALU.mult).then_inc(sem("gh"), 1)
                ve.wait_ge(sem("ge1"), i + 1)
                ve.tensor_sub(tdf[:, :], g1t[:, :], xnb[i % 2][:, :]).then_inc(sem("df"), 1)
                ve.tensor_add(trs[:, :], g1t[:, :], xnb[i % 2][:, :]).then_inc(sem("rs"), 1)
                ve.tensor_reduce(lnsml[0:64, 2:3], trs[:, :], AX.X, ALU.add)
                gap(ve)
                ve.tensor_scalar(lnsml[0:64, 3:4], lnsml[0:64, 2:3], 1.0 / H, None, ALU.mult)
                gap(ve)
                ve.tensor_tensor(lnsml[0:64, 4:5], lnsml[0:64, 3:4], lnsml[0:64, 3:4], ALU.mult)
                ve.wait_ge(sem("ln1"), i + 1)
                ve.tensor_scalar(lnsml[0:64, 5:6], lnsml[0:64, 1:2], 1.0 / H, EPS, ALU.mult, ALU.add)
                gap(ve)
                ve.tensor_tensor(lnsml[0:64, 6:7], lnsml[0:64, 5:6], lnsml[0:64, 4:5], ALU.subtract).then_inc(sem("ln2"), 1)
                ve.wait_ge(sem("ln3"), i + 1)
                recip1(ve, t8b, lnsml[0:64, 7:8], t8a, p=64)
                gap(ve)
                if i >= 2:
                    ve.wait_ge(sem("mail"), 32 * (i - 1))
                ve.tensor_scalar(ybuf[i % 2][:, :], trs[:, :], lnsml[0:64, 3:4], t8b[0:64, 0:1],
                                 ALU.subtract, ALU.mult).then_inc(sem("yy"), 1)
                ve.wait_ge(sem("tm"), i + 1)
                ve.tensor_copy(lsb[i % 2][0:1, 0:1], pS[0:1, 0:1]).then_inc(sem("lse"), 1)
                ve.wait_ge(sem("bc"), i + 1)
                ve.tensor_reduce(t8a[:, 0:1], pC[:, 0:8], AX.X, ALU.add)
                gap(ve)
                ve.tensor_scalar(t8b[:, 1:2], t8a[:, 0:1], -C_LOSS, 1.0, ALU.mult, ALU.add)
                gap(ve)
                ve.tensor_tensor(dbuf[(i + 1) % 2][:, 0:1], dbuf[i % 2][:, 0:1], t8b[:, 1:2], ALU.mult)
                gap(ve)
                ve.tensor_tensor(t8a[:, 1:2], dbuf[(i + 1) % 2][:, 0:1], dbuf[(i + 1) % 2][:, 0:1], ALU.mult)
                gap(ve)
                ve.tensor_tensor(d3buf[(i + 1) % 2][:, 0:1], t8a[:, 1:2], dbuf[(i + 1) % 2][:, 0:1], ALU.mult).then_inc(sem("d"), 1)
            bar(ve, 3)
            ve.memset(yAB[0][0:1, :], 0.0)
            ve.memset(mvp_s[:, :], 0.0)
            ve.memset(mvp_q[:, :], 0.0).then_inc(sem("z0v"), 1)
            ve.wait_ge(sem("ld0"), 16 * 4)
            ve.tensor_scalar(t8a[:, 3:4], mwc[0][:, 0:1], 1.0 / 127.0, None, ALU.mult)
            gap(ve)
            ve.tensor_scalar_mul(cw_s[:, :], cw_t[:, :], t8a[:, 3:4]).then_inc(sem("cws"), 1)
            TPN = [0]
            for b in range(8):
                ve.memset(rmax[:, :], 0.0)
                ve.memset(rmin[:, :], 0.0)
                if b >= 1:
                    ve.wait_ge(sem("up"), 32 * b)
                    ve.wait_ge(sem("sbT"), b)
                    ve.wait_ge(sem("sqT"), b)
                for j in range(4):
                    c = 4 * b + j
                    ve.wait_ge(sem("yl"), 16 * (c + 1))
                    ve.wait_ge(sem("hg"), 16 * (c + 1))
                    ve.tensor_reduce(ayt[0:66, 0:1], yt[c % 2][:, :], AX.X, ALU.max, apply_absolute_value=True)
                    gap(ve)
                    ve.tensor_scalar_max(ayt[0:66, 1:2], ayt[0:66, 0:1], 1e-5)
                    gap(ve)
                    ve.tensor_copy(aycb[0:66, 0:1], ayt[0:66, 1:2])
                    gap(ve)
                    ve.tensor_copy(ayf[0:66, 0:1], aycb[0:66, 0:1])
                    gap(ve)
                    ve.tensor_scalar_mul(t66a[0:66, 0:8], ones_col[0:66, 0:8], ayf[0:66, 0:1])
                    gap(ve)
                    ve.reciprocal(t66b[0:66, 0:8], t66a[0:66, 0:8])
                    gap(ve)
                    ve.tensor_scalar(sy66[0:66, 0:1], t66b[0:66, 0:1], 127.0, None, ALU.mult).then_inc(sem("q1"), 1)
                    for k in range(8):
                        T = TPN[0]
                        ve.wait_ge(sem("tpq"), T + 1)
                        dstv = yqT.rearrange("p (kk x) -> p kk x", kk=8)[:, k, j * 66:(j + 1) * 66]
                        ve.tensor_copy(dstv, TPB[T % 2][:, 0:66]).then_inc(sem("yqe"), 1)
                        TPN[0] += 1
                    T = TPN[0]
                    ve.wait_ge(sem("tpq"), T + 1)
                    ve.tensor_copy(mvp_s[0:1, j * 66:(j + 1) * 66], TPB[T % 2][0:1, 0:66]).then_inc(sem("yqe"), 1)
                    TPN[0] += 1
                ve.wait_ge(sem("sbT"), b + 1)
                ve.tensor_copy(sinv_bc[:, :], pA[:, 0:264]).then_inc(sem("sb"), 1)
                ve.wait_ge(sem("cws"), 1)
                for m in range(32):
                    ve.wait_ge(sem("up"), 32 * b + m + 1)
                    bank = pE if m % 2 == 0 else pF
                    ve.tensor_tensor(htmp[m % 2][:, :], bank[:, 0:264], sinv_bc[:, :], ALU.mult).then_inc(sem("hT"), 1)
                    ve.wait_ge(sem("cva"), 32 * b + m + 1)
                    wa = cw_s[:, 3 * m + 0:3 * m + 1]
                    wc = cw_s[:, 3 * m + 2:3 * m + 3]
                    h4 = htmp[m % 2].rearrange("p (g x) -> p g x", g=4)
                    A4 = cvA.rearrange("p (g x) -> p g x", g=4)
                    B4 = cvB.rearrange("p (g x) -> p g x", g=4)
                    C4 = cvC.rearrange("p (g x) -> p g x", g=4)
                    ve.scalar_tensor_tensor(B4[:, :, 2:3], h4[:, :, 0:1], wa, A4[:, :, 2:3], ALU.mult, ALU.add)
                    ve.scalar_tensor_tensor(B4[:, :, 3:66], h4[:, :, 2:65], wa, A4[:, :, 3:66], ALU.mult, ALU.add)
                    ve.scalar_tensor_tensor(C4[:, :, 2:65], h4[:, :, 3:66], wc, B4[:, :, 2:65], ALU.mult, ALU.add)
                    ve.scalar_tensor_tensor(C4[:, :, 65:66], h4[:, :, 1:2], wc, B4[:, :, 65:66], ALU.mult, ALU.add).then_inc(sem("cvd"), 1)
                    ve.wait_ge(sem("sil"), 32 * b + m + 1)
                    ve.tensor_tensor(rmax[:, :], rmax[:, :], hsb[:, m * 256:(m + 1) * 256], ALU.max)
                    ve.tensor_tensor(rmin[:, :], rmin[:, :], hsb[:, m * 256:(m + 1) * 256], ALU.min).then_inc(sem("cvv"), 1)
                ve.tensor_scalar(rmin[:, :], rmin[:, :], -1.0, None, ALU.mult)
                ve.tensor_tensor(rmax[:, :], rmax[:, :], rmin[:, :], ALU.max)
                ve.tensor_copy(rmaxb[:, :], rmax[:, :]).then_inc(sem("rmb"), 1)
                for q in range(2):
                    T = TPN[0]
                    ve.wait_ge(sem("tpq"), T + 1)
                    ve.tensor_reduce(amaxh[:, q:q + 1], TPB[T % 2][:, 66:194], AX.X, ALU.max).then_inc(sem("yqe"), 1)
                    TPN[0] += 1
                gap(ve)
                ve.tensor_scalar_max(amaxh[:, 2:4], amaxh[:, 0:2], 1e-5)
                gap(ve)
                for q in range(2):
                    recip1(ve, t8b, amaxh[:, 2 + q:3 + q], t8a)
                    gap(ve)
                    ve.tensor_scalar(t8a[:, 4:5], t8b[:, 0:1], 127.0, None, ALU.mult)
                    gap(ve)
                    ve.tensor_copy(shb16[:, q:q + 1], t8a[:, 4:5])
                    gap(ve)
                ve.tensor_copy(shbf[:, 0:2], shb16[:, 0:2])
                gap(ve)
                for q in range(2):
                    recip1(ve, t8b, shbf[:, q:q + 1], t8a)
                    gap(ve)
                    r = ve.tensor_scalar_mul(shv[:, q:q + 1], t8b[:, 0:1], mwc[1][:, 0:1])
                    gap(ve)
                r.then_inc(sem("shc"), 1)
                for q in range(2):
                    T = TPN[0]
                    ve.wait_ge(sem("tpq"), T + 1)
                    ve.tensor_copy(mvp_q[0:1, q * 128:(q + 1) * 128], TPB[T % 2][0:1, 194:322]).then_inc(sem("yqe"), 1)
                    TPN[0] += 1
                ve.wait_ge(sem("sqT"), b + 1)
                ve.tensor_copy(sq_bc[:, :], pB[:, 0:256]).then_inc(sem("amQ"), 1)
                for m in range(32):
                    if m >= 1:
                        ve.wait_ge(sem("qa"), 32 * b + m)
                    ve.tensor_tensor(qtmp, hsb[:, m * 256:(m + 1) * 256], sq_bc[:, :], ALU.mult).then_inc(sem("qv"), 1)
                for p2 in range(2):
                    gp_ = 2 * b + p2
                    ve.wait_ge(sem("de1"), gp_ + 1)
                    ve.wait_ge(sem("yab"), 16 * (gp_ + 1))
                    ve.tensor_add(outt[:, :], outt[:, :], yAB[gp_ % 2][:, :]).then_inc(sem("rv0"), 1)
                    ve.tensor_reduce(lnsml2[:, 2:3], outt[:, :], AX.X, ALU.add)
                    gap(ve)
                    ve.tensor_scalar(lnsml2[:, 3:4], lnsml2[:, 2:3], 1.0 / H, None, ALU.mult)
                    gap(ve)
                    ve.tensor_tensor(lnsml2[:, 4:5], lnsml2[:, 3:4], lnsml2[:, 3:4], ALU.mult)
                    ve.wait_ge(sem("ra1"), gp_ + 1)
                    ve.tensor_scalar(lnsml2[:, 5:6], lnsml2[:, 1:2], 1.0 / H, EPS, ALU.mult, ALU.add)
                    gap(ve)
                    ve.tensor_tensor(lnsml2[:, 6:7], lnsml2[:, 5:6], lnsml2[:, 4:5], ALU.subtract).then_inc(sem("rv2"), 1)
                    ve.wait_ge(sem("ra2"), gp_ + 1)
                    recip1(ve, t8b, lnsml2[:, 7:8], t8a)
                    gap(ve)
                    ve.tensor_scalar(yAB[gp_ % 2][:, :], outt[:, :], lnsml2[:, 3:4], t8b[:, 0:1],
                                     ALU.subtract, ALU.mult).then_inc(sem("ot"), 1)
            bar(ve, 4)

        @block.scalar
        def _(sc):
            n = 0
            for g in range(4):
                for uvi in range(2):
                    for m in range(32):
                        sc.wait_ge(sem("mm0"), n + 1)
                        if n >= 2:
                            if n % 2 == 0:
                                sc.wait_ge(sem("st0a"), 16 * (n // 2))
                            else:
                                sc.wait_ge(sem("st0b"), 16 * (n // 2))
                        bank = pA if n % 2 == 0 else pB
                        sc.copy(uvtmp[n % 2][:, :], bank[:, :]).then_inc(sem("ev0"), 1)
                        if n % 2 == 1:
                            src = uvtmp[1].rearrange("p (c t) -> p c t", c=8)
                            dst = uv_hbm[uvi, 8 * g:8 * g + 8, :, m * 64:(m + 1) * 64].rearrange("c p t -> p c t")
                            sc.dma_start(out=dst, in_=src).then_inc(sem("st0b"), 16)
                        n += 1
            bar(sc, 1)
            for t in range(NW):
                sc.wait_ge(sem("tclip"), t + 1)
                sc.copy(itmp[:, :], ftmp[:, :]).then_inc(sem("tcast"), 1)
                if t < 8:
                    sc.copy(w1qT[:, t * FH:(t + 1) * FH], itmp[:, :]).then_inc(sem("tcast"), 1)
                else:
                    if t > 8:
                        sc.wait_ge(sem("w2st"), 16 * (t - 8))
                    sc.copy(bfv[:, :], itmp[:, :]).then_inc(sem("tcast"), 1)
            bar(sc, 2)
            for i in range(NCH):
                sc.wait_ge(sem("d"), i + 1)
                sc.wait_ge(sem("uvl"), 48 * (i + 1))
                for hf in range(2):
                    if 2 * i + hf >= 1:
                        sc.wait_ge(sem("gh"), 2 * i + hf)
                    sc.activation(tsig[:, :], ubuf[i % 2][:, hf * 1024:(hf + 1) * 1024], AF.Sigmoid,
                                  scale=dbuf[i % 2][:, 0:1]).then_inc(sem("sigh"), 1)
                sc.wait_ge(sem("mm1"), i + 1)
                d3ap = d3buf[i % 2][0:64, 0:1]
                sc.activation(g1t[:, 0:512], pA[0:64, :], AF.Identity, scale=d3ap)
                sc.activation(g1t[:, 512:1024], pB[0:64, :], AF.Identity, scale=d3ap).then_inc(sem("ge1"), 1)
                sc.wait_ge(sem("df"), i + 1)
                sc.activation(gbf[0:64, 0:1024], tdf[:, :], AF.Square, accum_out=lnsml[0:64, 0:1]).then_inc(sem("lq"), 1)
                sc.wait_ge(sem("rs"), i + 1)
                sc.activation(gbf[0:64, 0:1024], trs[:, :], AF.Square, accum_out=lnsml[0:64, 1:2]).then_inc(sem("ln1"), 1)
                sc.wait_ge(sem("ln2"), i + 1)
                sc.activation(lnsml[0:64, 7:8], lnsml[0:64, 6:7], AF.Sqrt).then_inc(sem("ln3"), 1)
            bar(sc, 3)
            for b in range(8):
                for j in range(4):
                    c = 4 * b + j
                    sc.wait_ge(sem("q1"), c + 1)
                    if c >= 1:
                        bb, jj = (c - 1) // 4, (c - 1) % 4
                        sc.wait_ge(sem("tpq"), 40 * bb + 9 * (jj + 1))
                    for hf in range(2):
                        sc.activation(yqi[:, :], yt[c % 2][:, hf * 512:(hf + 1) * 512], AF.Identity,
                                      scale=sy66[0:66, 0:1])
                        a2 = sc.activation(yqb[:, hf * 512:(hf + 1) * 512], yqi[:, :], AF.Identity)
                    a2.then_inc(sem("q2"), 1)
                for m in range(32):
                    sc.wait_ge(sem("hT"), 32 * b + m + 1)
                    if m >= 1:
                        sc.wait_ge(sem("cvd"), 32 * b + m)
                    sc.activation(cvA[:, :], htmp[m % 2][:, :], AF.Identity,
                                  scale=cw_s[:, 3 * m + 1:3 * m + 2], bias=cb_t[:, m:m + 1]).then_inc(sem("cva"), 1)
                    sc.wait_ge(sem("cvd"), 32 * b + m + 1)
                    c4 = cvC.rearrange("p (g x) -> p g x", g=4)[:, :, 2:66]
                    s4 = hsb[:, m * 256:(m + 1) * 256].rearrange("p (g x) -> p g x", g=4)
                    sc.activation(s4, c4, AF.Silu).then_inc(sem("sil"), 1)
                if b >= 1:
                    sc.wait_ge(sem("dn"), 2 * b)
                for m in range(32):
                    sc.wait_ge(sem("qv"), 32 * b + m + 1)
                    sc.copy(hint[:, :], qtmp).then_inc(sem("qa"), 1)
                    sc.copy(hqT[:, m * 256:(m + 1) * 256], hint[:, :]).then_inc(sem("qa2"), 1)
                for p2 in range(2):
                    gp_ = 2 * b + p2
                    sc.wait_ge(sem("dn"), gp_ + 1)
                    if gp_ >= 1:
                        sc.wait_ge(sem("ot"), gp_)
                    shp = shv[:, p2:p2 + 1]
                    sc.activation(outt[:, 0:512], pC[:, :], AF.Identity, scale=shp)
                    sc.activation(outt[:, 512:1024], pD[:, :], AF.Identity, scale=shp).then_inc(sem("de1"), 1)
                    sc.wait_ge(sem("rv0"), gp_ + 1)
                    sc.activation(yAB[gp_ % 2][:, :], outt[:, :], AF.Square, accum_out=lnsml2[:, 1:2]).then_inc(sem("ra1"), 1)
                    sc.wait_ge(sem("rv2"), gp_ + 1)
                    sc.activation(lnsml2[:, 7:8], lnsml2[:, 6:7], AF.Sqrt).then_inc(sem("ra2"), 1)
            bar(sc, 4)

    return nc


def kernel(L0_attn_x, L0_attn_w0, L0_attn_w1, L0_attn_w2,
           L0_ffn_w1, L0_ffn_conv_w, L0_ffn_conv_b, L0_ffn_w2):
    x = np.ascontiguousarray(np.asarray(L0_attn_x, np.float32)[0])
    w0 = np.ascontiguousarray(np.asarray(L0_attn_w0, np.float32))
    w1 = np.ascontiguousarray(np.asarray(L0_attn_w1, np.float32))
    w2 = np.ascontiguousarray(np.asarray(L0_attn_w2, np.float32))
    fw1 = np.asarray(L0_ffn_w1, np.float32)
    fw2 = np.asarray(L0_ffn_w2, np.float32)
    cw = np.asarray(L0_ffn_conv_w, np.float32)
    cb = np.asarray(L0_ffn_conv_b, np.float32)

    fw1T = np.ascontiguousarray(fw1.T)
    fw2T = np.ascontiguousarray(fw2.T)
    cwv = np.ascontiguousarray(cw[:, 0, :].reshape(32, 128, 3).transpose(1, 0, 2).reshape(128, 96))
    cbv = np.ascontiguousarray(cb.reshape(32, 128).T)
    idnb = np.eye(128, dtype=ml_dtypes.bfloat16)

    nc = build_nc()
    in_maps = []
    tok_rows = []
    for c in range(8):
        rows = (np.arange(NCH)[:, None] * CHT + 64 * c + np.arange(64)[None, :]).reshape(-1)
        tok_rows.append(rows)
        xn = np.ascontiguousarray(x[rows])
        xt = np.ascontiguousarray(xn.T)
        hidx = np.zeros((2, NCH), np.uint32)
        for j in range(NCH):
            if c > 0:
                hidx[0, j] = ((j + 1) * 8 + (c - 1)) * 3 + 1
            elif j > 0:
                hidx[0, j] = (j * 8 + 7) * 3 + 1
            else:
                hidx[0, j] = ZROW
            if c < 7:
                hidx[1, j] = ((j + 1) * 8 + (c + 1)) * 3 + 0
            elif j < NCH - 1:
                hidx[1, j] = ((j + 2) * 8 + 0) * 3 + 0
            else:
                hidx[1, j] = ZROW
        in_maps.append({
            "xn": xn, "xt": xt, "w0": w0, "w2": w2, "w1": w1,
            "fw1T": fw1T, "fw2T": fw2T, "cwv": cwv, "cbv": cbv,
            "idnb": idnb, "hidx": hidx,
        })

    res = run_bass_kernel_spmd(nc, in_maps, core_ids=list(range(8)))
    out = np.empty((S, H), np.float32)
    for c in range(8):
        out[tok_rows[c]] = res.results[c]["o"]
    return out[None]


# version 07
# speedup vs baseline: 1.0553x; 1.0553x over previous
import numpy as np
import ml_dtypes
from contextlib import ExitStack

import concourse.bass as bass
import concourse.mybir as mybir
from concourse.bass_utils import run_bass_kernel_spmd
from concourse.bass_core_rust import IndirectOffsetOnAxis

F32 = mybir.dt.float32
BF16 = mybir.dt.bfloat16
I32 = mybir.dt.int32
U32 = mybir.dt.uint32
AF = mybir.ActivationFunctionType
ALU = mybir.AluOpType
AX = mybir.AxisListType

S, H, FH = 16384, 1024, 4096
NCH, CHT, TPC = 32, 512, 64
NTOK = NCH * TPC
LR = 1e-3
EPS = 1e-5
RG = [[0, 1, 2, 3, 4, 5, 6, 7]]
SLAB = 3072
ZROW = 528
C_LOSS = LR / (CHT * H)
WCNT = float(H * FH)
NW = 16


def build_nc():
    nc = bass.Bass("TRN2", target_bir_lowering=False)

    xn_d = nc.dram_tensor("xn", [NTOK, H], F32, kind="ExternalInput").ap()
    xt_d = nc.dram_tensor("xt", [H, NTOK], F32, kind="ExternalInput").ap()
    w0_d = nc.dram_tensor("w0", [H, FH], F32, kind="ExternalInput").ap()
    w2_d = nc.dram_tensor("w2", [H, FH], F32, kind="ExternalInput").ap()
    w1_d = nc.dram_tensor("w1", [FH, H], F32, kind="ExternalInput").ap()
    fw1T_d = nc.dram_tensor("fw1T", [H, FH], F32, kind="ExternalInput").ap()
    fw2T_d = nc.dram_tensor("fw2T", [FH, H], F32, kind="ExternalInput").ap()
    cwv_d = nc.dram_tensor("cwv", [128, 96], F32, kind="ExternalInput").ap()
    cbv_d = nc.dram_tensor("cbv", [128, 32], F32, kind="ExternalInput").ap()
    idnb_d = nc.dram_tensor("idnb", [128, 128], BF16, kind="ExternalInput").ap()
    hidx_d = nc.dram_tensor("hidx", [2, NCH], U32, kind="ExternalInput").ap()
    o_d = nc.dram_tensor("o", [NTOK, H], F32, kind="ExternalOutput").ap()

    uv_hbm = nc.dram_tensor("uv_hbm", [2, NCH, 128, NTOK], BF16, kind="Internal").ap()
    y_hbm = nc.dram_tensor("y_hbm", [NCH, TPC, H], F32, kind="Internal").ap()
    w2q_hbm = nc.dram_tensor("w2q_hbm", [128, 32768], BF16, kind="Internal").ap()
    agy_in = nc.dram_tensor("agy_in", [33, 2048], F32, kind="Internal").ap()
    agl_in = nc.dram_tensor("agl_in", [33, 1], F32, kind="Internal").ap()
    lossag = nc.dram_tensor("lossag", [33, 8], F32, kind="Internal",
                            addr_space="Shared").ap()
    hal_all = nc.dram_tensor("hal_all", [529, 1024], F32, kind="Internal",
                             addr_space="Shared").ap()

    ctx = ExitStack()
    with ctx:
        e = ctx.enter_context
        SB_BYTES = 212736
        arena_t = e(nc.sbuf_tensor("arena", [128, SB_BYTES], mybir.dt.uint8))
        hmap = {F32: arena_t.bitcast(F32), BF16: arena_t.bitcast(BF16),
                I32: arena_t.bitcast(I32), U32: arena_t.bitcast(U32)}
        esz = {F32: 4, BF16: 2, I32: 4, U32: 4}
        cur = [0]

        def take(dt, p, n):
            off = (cur[0] + 63) // 64 * 64
            cur[0] = off + n * esz[dt]
            assert cur[0] <= SB_BYTES, f"arena overflow {cur[0]}"
            return hmap[dt][0:p, off // esz[dt]: off // esz[dt] + n]

        def at(dt, off_b, p, n):
            assert off_b % 64 == 0
            return hmap[dt][0:p, off_b // esz[dt]: off_b // esz[dt] + n]

        ones_col = take(F32, 128, 8)
        E128 = take(F32, 128, 128)
        ones128 = take(F32, 128, 128)
        idn_b = take(BF16, 128, 128)
        cw_t = take(F32, 128, 96)
        cb_t = take(F32, 128, 32)
        dbuf = [take(F32, 128, 8) for _ in range(2)]
        d3buf = [take(F32, 128, 8) for _ in range(2)]
        accw = [[take(F32, 128, 8) for _ in range(2)] for _ in range(2)]
        mwc = [take(F32, 128, 8) for _ in range(2)]
        swq = [take(F32, 128, 8) for _ in range(2)]
        t8a = take(F32, 128, 8)
        t8b = take(F32, 128, 8)
        lsb = [take(F32, 1, 8) for _ in range(2)]
        lrp = take(F32, 128, 8)
        pad8 = take(F32, 128, 8)
        sidx = take(U32, 2, NCH)
        PM = (cur[0] + 63) // 64 * 64

        cur[0] = PM
        w0bf = take(BF16, 128, 8 * FH)
        w2bf = take(BF16, 128, 8 * FH)
        xtbf = take(BF16, 128, 8 * NTOK)
        uvtmp = [take(BF16, 128, 512) for _ in range(2)]
        fwbuf = [take(F32, 128, FH) for _ in range(2)]
        assert cur[0] <= SB_BYTES, f"P0 {cur[0]}"

        REGA = PM
        W1QT = PM + 65536
        POST = PM + 131072
        w1bf = at(BF16, REGA, 128, 32 * 1024)
        w2qT = at(BF16, REGA, 128, 32 * 1024)
        w1qT = at(BF16, W1QT, 128, 8 * FH)

        cur[0] = POST
        fsb = [take(F32, 128, FH) for _ in range(2)]
        ftmp = take(F32, 128, FH)
        itmp = take(I32, 128, FH)
        FSB_OFF = (POST + 63) // 64 * 64
        FT_OFF = (FSB_OFF + 2 * FH * 4 + 63) // 64 * 64
        bfv = at(BF16, FT_OFF, 128, FH)
        assert cur[0] <= SB_BYTES, f"P0b {cur[0]}"

        cur[0] = POST
        ubuf = [take(BF16, 128, NTOK) for _ in range(3)]
        vbuf = [take(BF16, 128, NTOK) for _ in range(2)]
        tsig = take(F32, 128, 1024)
        tt2 = take(F32, 128, 1024)
        gbf = take(BF16, 128, NTOK)
        xnb = [take(F32, 64, H) for _ in range(2)]
        g1t = take(F32, 64, H)
        tdf = tsig[0:64, :]
        trs = take(F32, 64, H)
        ybuf = [take(F32, 64, H) for _ in range(2)]
        lnsml = take(F32, 128, 16)
        assert cur[0] <= SB_BYTES, f"P1 {cur[0]}"

        cur[0] = POST
        yt = [take(F32, 66, H) for _ in range(2)]
        yqi = take(I32, 66, 512)
        yqb = take(BF16, 66, H)
        yqT = take(BF16, 128, 8 * 264)
        yAB = [take(F32, 128, H) for _ in range(2)]
        hsb = take(BF16, 128, 32 * 256)
        hqT = take(BF16, 128, 32 * 256)
        htmp = [take(F32, 128, 264) for _ in range(2)]
        cvA = take(F32, 128, 264)
        cvB = take(F32, 128, 264)
        cvC = take(F32, 128, 264)
        sinv_bc = take(F32, 128, 264)
        sq_bc = take(F32, 128, 256)
        hint = take(I32, 128, 256)
        rmax = take(F32, 128, 256)
        rmin = take(F32, 128, 256)
        rmaxb = take(BF16, 128, 256)
        ayt = take(F32, 66, 8)
        aycb = take(BF16, 66, 8)
        ayf = take(F32, 66, 8)
        sy66 = take(F32, 66, 8)
        t66a = take(F32, 66, 8)
        t66b = take(F32, 66, 8)
        mvp_s = take(F32, 128, 264)
        mvp_q = take(F32, 128, 256)
        amaxh = take(F32, 128, 8)
        shb16 = take(BF16, 128, 8)
        shbf = take(F32, 128, 8)
        shv = take(F32, 128, 8)
        cw_s = take(F32, 128, 96)
        outt = take(F32, 128, H)
        lnsml2 = take(F32, 128, 16)
        assert cur[0] <= SB_BYTES, f"P2 {cur[0]}"
        qtmp = cvA[:, 0:256]

        pA = e(nc.psum_tensor("pA", [128, 512], F32))
        pB = e(nc.psum_tensor("pB", [128, 512], F32))
        pC = e(nc.psum_tensor("pC", [128, 512], F32))
        pD = e(nc.psum_tensor("pD", [128, 512], F32))
        pE = e(nc.psum_tensor("pE", [128, 512], F32))
        pF = e(nc.psum_tensor("pF", [128, 512], F32))
        pS = e(nc.psum_tensor("pS", [128, 512], F32))
        pT = e(nc.psum_tensor("pT", [128, 1024], BF16))
        TPB = [pT[:, 0:1024], pS.bitcast(BF16)[:, 0:1024]]

        sems = {}

        def sem(name):
            if name not in sems:
                sems[name] = e(nc.semaphore(name))
            return sems[name]

        ph = sem("ph")

        def bar(eng, k, drain=True):
            if drain:
                eng.drain()
            eng.nop().then_inc(ph, 1)
            eng.wait_ge(ph, 5 * k)

        def gap(vec):
            vec.tensor_tensor(pad8[:, 0:1], pad8[:, 0:1], ones_col[:, 0:1], ALU.add)

        def recip1(vec, out8, in1ap, tmp8, p=128):
            vec.tensor_scalar_mul(tmp8[0:p, 0:8], ones_col[0:p, 0:8], in1ap)
            gap(vec)
            return vec.reciprocal(out8[0:p, 0:8], tmp8[0:p, 0:8])

        def fw_tiles(t, buf):
            if t < 8:
                return [(buf[:, :], fw1T_d[t * 128:(t + 1) * 128, :])]
            k4 = t - 8
            return [(buf[:, a * 1024:(a + 1) * 1024],
                     fw2T_d[k4 * 512 + a * 128:k4 * 512 + (a + 1) * 128, :]) for a in range(4)]

        FWCNT = [16 if _t < 8 else 64 for _t in range(NW)]
        def parity_cum(par):
            out = {}
            c = 0
            for _t in range(par, NW, 2):
                c += FWCNT[_t]
                out[_t] = c
            return out
        CUME, CUMO = parity_cum(0), parity_cum(1)

        block = e(nc.Block())

        @block.sync
        def _(sy):
            sy.dma_start(out=cw_t[:, :], in_=cwv_d[:, :]).then_inc(sem("ld0"), 16)
            sy.dma_start(out=cb_t[:, :], in_=cbv_d[:, :]).then_inc(sem("ld0"), 16)
            sy.dma_start(out=idn_b[:, :], in_=idnb_d[:, :]).then_inc(sem("ld0"), 16)
            sy.dma_start(out=sidx[:, :], in_=hidx_d[:, :]).then_inc(sem("ld0"), 16)
            for pr in fw_tiles(0, fwbuf[0]):
                sy.dma_start(out=pr[0], in_=pr[1]).then_inc(sem("fw0E"), 16)
            for n in range(0, 256, 2):
                g, uvi, m = n // 64, (n // 32) % 2, n % 32
                if n % 16 == 0 and 2 + n // 16 < NW and (2 + n // 16) % 2 == 0:
                    t = 2 + n // 16
                    sy.wait_ge(sem("mn"), t - 1)
                    for pr in fw_tiles(t, fwbuf[t % 2]):
                        sy.dma_start(out=pr[0], in_=pr[1]).then_inc(sem("fw0E"), 16)
                sy.wait_ge(sem("ev0"), n + 1)
                src = uvtmp[0].rearrange("p (c t) -> p c t", c=8)
                dst = uv_hbm[uvi, 8 * g:8 * g + 8, :, m * 64:(m + 1) * 64].rearrange("c p t -> p c t")
                sy.dma_start(out=dst, in_=src).then_inc(sem("st0a"), 16)
            sy.wait_ge(sem("st0a"), 16 * 128)
            sy.wait_ge(sem("st0b"), 16 * 128)
            bar(sy, 1, drain=False)
            for t0 in (0, 1):
                for pr in fw_tiles(t0, fsb[t0 % 2]):
                    sy.dma_start(out=pr[0], in_=pr[1]).then_inc(sem("fwbE") if t0 % 2 == 0 else sem("fwbO"), 16)
            for t in range(3, NW, 2):
                sy.wait_ge(sem("tclip"), t - 1)
                for pr in fw_tiles(t, fsb[t % 2]):
                    sy.dma_start(out=pr[0], in_=pr[1]).then_inc(sem("fwbO"), 16)
            sy.wait_ge(sem("w2st"), 16 * 8)
            bar(sy, 2, drain=False)
            def uld(i):
                sy.dma_start(out=ubuf[i % 3][:, :], in_=uv_hbm[0, i, :, :]).then_inc(sem("uvl"), 16)
            def xld(i):
                sy.dma_start(out=xnb[i % 2][:, :], in_=xn_d[i * 64:(i + 1) * 64, :]).then_inc(sem("xvl"), 16)
            for j in range(3):
                uld(j)
            for j in range(2):
                xld(j)
            for i in range(NCH):
                if i + 2 < NCH:
                    sy.wait_ge(sem("rs"), i + 1)
                    xld(i + 2)
                if i + 3 < NCH:
                    sy.wait_ge(sem("gh"), 2 * (i + 1))
                    uld(i + 3)
                sy.wait_ge(sem("yy"), i + 1)
                sy.dma_start(out=y_hbm[i, :, :], in_=ybuf[i % 2][:, :]).then_inc(sem("yst"), 16)
            sy.wait_ge(sem("yst"), 16 * NCH)
            bar(sy, 3, drain=False)
            for q in range(4):
                sy.dma_start(out=w2qT[:, q * 8192:(q + 1) * 8192], in_=w2q_hbm[:, q * 8192:(q + 1) * 8192]).then_inc(sem("w2l"), 16)
            sy.wait_ge(sem("z0v"), 1)
            sy.dma_start(out=hal_all[528:529, :], in_=yAB[0][0:1, :]).then_inc(sem("z0"), 16)
            for b in range(8):
                for j in range(4):
                    c = 4 * b + j
                    if c >= 2:
                        sy.wait_ge(sem("q2"), c - 1)
                    sy.dma_start(out=yt[c % 2][2:66, :], in_=y_hbm[c, :, :]).then_inc(sem("yl"), 16)
                for p2 in range(2):
                    gp_ = 2 * b + p2
                    cA = 4 * b + 2 * p2
                    if gp_ >= 2:
                        sy.wait_ge(sem("ost"), 16 * (gp_ - 1))
                    sy.dma_start(out=yAB[gp_ % 2][:, :], in_=y_hbm[cA:cA + 2, :, :].rearrange("a b c -> (a b) c")).then_inc(sem("yab"), 16)
                    sy.wait_ge(sem("ot"), gp_ + 1)
                    sy.dma_start(out=o_d[64 * cA:64 * cA + 128, :], in_=yAB[gp_ % 2][:, :]).then_inc(sem("ost"), 16)
            sy.wait_ge(sem("ost"), 16 * 16)
            bar(sy, 4, drain=False)

        @block.gpsimd
        def _(gp):
            for k in range(8):
                gp.dma_start(out=xtbf[:, k * NTOK:(k + 1) * NTOK], in_=xt_d[k * 128:(k + 1) * 128, :]).then_inc(sem("ldw"), 16)
            for k in range(8):
                gp.dma_start(out=w0bf[:, k * FH:(k + 1) * FH], in_=w0_d[k * 128:(k + 1) * 128, :]).then_inc(sem("ldw"), 16)
            for k in range(8):
                gp.dma_start(out=w2bf[:, k * FH:(k + 1) * FH], in_=w2_d[k * 128:(k + 1) * 128, :]).then_inc(sem("ldw"), 16)
            bar(gp, 1)
            for k in range(32):
                gp.dma_start(out=w1bf[:, k * 1024:(k + 1) * 1024], in_=w1_d[k * 128:(k + 1) * 128, :]).then_inc(sem("w1l"), 16)
            for t in range(8, 16):
                gp.wait_ge(sem("tcast"), 2 * (t + 1))
                gp.dma_start(out=w2q_hbm[:, (t - 8) * 4096:(t - 7) * 4096], in_=bfv[:, :]).then_inc(sem("w2st"), 16)
            bar(gp, 2)
            for i in range(NCH):
                if i >= 1:
                    gp.wait_ge(sem("yy"), i)
                    gp.dma_start(out=agy_in[i:i + 1, 0:1024], in_=ybuf[(i - 1) % 2][0:1, :]).then_inc(sem("mail"), 16)
                    gp.dma_start(out=agy_in[i:i + 1, 1024:2048], in_=ybuf[(i - 1) % 2][63:64, :]).then_inc(sem("mail"), 16)
                gp.wait_ge(sem("lse"), i + 1)
                gp.dma_start(out=agl_in[i:i + 1, 0:1], in_=lsb[i % 2][0:1, 0:1]).then_inc(sem("lms"), 16)
                gp.wait_ge(sem("lms"), 16 * (i + 1))
                gp.collective_compute(
                    "AllGather", ALU.bypass, RG,
                    ins=[agl_in[i:i + 1, 0:1]],
                    outs=[lossag[i:i + 1, 0:8]],
                ).then_inc(sem("agd"), 1)
                gp.wait_ge(sem("agd"), i + 1)
                if i >= 1:
                    gp.wait_ge(sem("bc"), i)
                gp.dma_start(out=lrp[0:1, 0:8], in_=lossag[i:i + 1, 0:8]).then_inc(sem("lrq"), 16)
            gp.wait_ge(sem("yy"), NCH)
            gp.dma_start(out=agy_in[32:33, 0:1024], in_=ybuf[31 % 2][0:1, :]).then_inc(sem("mail"), 16)
            gp.dma_start(out=agy_in[32:33, 1024:2048], in_=ybuf[31 % 2][63:64, :]).then_inc(sem("mail"), 16)
            gp.wait_ge(sem("mail"), 16 * 64)
            gp.collective_compute(
                "AllGather", ALU.bypass, RG,
                ins=[agy_in[0:33, :].rearrange("(o a) b -> o (a b)", o=1)],
                outs=[hal_all[0:528, :].rearrange("(o a) b -> o (a b)", o=1)],
            ).then_inc(sem("agd"), 1)
            bar(gp, 3)
            gp.wait_ge(sem("agd"), NCH + 1)
            gp.wait_ge(sem("z0"), 16)
            gp.wait_ge(sem("ld0"), 16 * 4)
            for c in range(NCH):
                if c >= 2:
                    gp.wait_ge(sem("q2"), c - 1)
                gp.indirect_dma_start(
                    out=yt[c % 2][0:2, :], out_offset=None,
                    in_=hal_all, in_offset=IndirectOffsetOnAxis(sidx[0:2, c:c + 1], 0),
                ).then_inc(sem("hg"), 16)
            bar(gp, 4)

        @block.tensor
        def _(tn):
            tn.wait_ge(sem("ldw"), 16 * 24)
            n = 0
            for g in range(4):
                for uvi in range(2):
                    wsrc = w0bf if uvi == 0 else w2bf
                    for m in range(32):
                        if n >= 2:
                            tn.wait_ge(sem("ev0"), n - 1)
                        bank = pA if n % 2 == 0 else pB
                        for k in range(8):
                            mm = tn.matmul(bank[:, :], wsrc[:, k * FH + m * 128:k * FH + (m + 1) * 128],
                                           xtbf[:, k * NTOK + g * 512:k * NTOK + (g + 1) * 512],
                                           start=(k == 0), stop=(k == 7))
                        mm.then_inc(sem("mm0"), 1)
                        n += 1
            tn.wait_ge(sem("mn"), NW + 1)
            for wi in range(2):
                mm = tn.matmul(pS[:, wi:wi + 1], ones128[:, :], accw[wi][0][:, 0:1], start=True, stop=True)
            mm.then_inc(sem("sw1"), 1)
            bar(tn, 1)
            bar(tn, 2)
            tn.wait_ge(sem("w1l"), 16 * 32)
            for i in range(NCH):
                tn.wait_ge(sem("gh"), 2 * (i + 1))
                if i >= 1:
                    tn.wait_ge(sem("ge1"), i)
                for ng in range(2):
                    bank = pA if ng == 0 else pB
                    for k in range(32):
                        mm = tn.matmul(bank[0:64, :], gbf[:, k * 64:(k + 1) * 64],
                                       w1bf[:, k * 1024 + ng * 512:k * 1024 + (ng + 1) * 512],
                                       start=(k == 0), stop=(k == 31))
                mm.then_inc(sem("mm1"), 1)
                tn.wait_ge(sem("lq"), i + 1)
                if i >= 1:
                    tn.wait_ge(sem("lse"), i)
                tn.matmul(pS[0:1, 0:1], ones_col[:, 0:1], lnsml[:, 0:1], start=True, stop=True).then_inc(sem("tm"), 1)
                tn.wait_ge(sem("lrq"), 16 * (i + 1))
                tn.wait_ge(sem("d"), i + 1)
                tn.matmul(pC[:, 0:8], E128[:, :], lrp[:, 0:8], start=True, stop=True).then_inc(sem("bc"), 1)
            bar(tn, 3)
            tn.wait_ge(sem("ld0"), 16 * 4)
            tn.wait_ge(sem("w2l"), 16 * 4)
            TPN = [0]

            def tpose(p, cols, in_ap, idn_ap):
                T = TPN[0]
                if T >= 2:
                    tn.wait_ge(sem("yqe"), T - 1)
                tn.transpose(TPB[T % 2][0:p, cols[0]:cols[1]], in_ap, idn_ap).then_inc(sem("tpq"), 1)
                TPN[0] += 1

            for b in range(8):
                for j in range(4):
                    c = 4 * b + j
                    tn.wait_ge(sem("q2"), c + 1)
                    for k in range(8):
                        tpose(128, (0, 66), yqb[:, k * 128:(k + 1) * 128], idn_b[0:66, 0:66])
                    tn.wait_ge(sem("q1"), c + 1)
                    tpose(1, (0, 66), aycb[0:66, 0:1], idn_b[0:66, 0:66])
                tn.wait_ge(sem("yqe"), 40 * b + 36)
                if b >= 1:
                    tn.wait_ge(sem("sb"), b)
                tn.matmul(pA[:, 0:264], E128[:, :], mvp_s[:, 0:264], start=True, stop=True).then_inc(sem("sbT"), 1)
                if b >= 1:
                    tn.wait_ge(sem("shc"), b)
                for m in range(32):
                    if m >= 2:
                        tn.wait_ge(sem("hT"), 32 * b + m - 1)
                    bank = pE if m % 2 == 0 else pF
                    for k in range(8):
                        mm = tn.matmul(bank[:, 0:264], w1qT[:, k * FH + m * 128:k * FH + (m + 1) * 128],
                                       yqT[:, k * 264:(k + 1) * 264], start=(k == 0), stop=(k == 7))
                    mm.then_inc(sem("up"), 1)
                tn.wait_ge(sem("rmb"), b + 1)
                tn.wait_ge(sem("hT"), 32 * (b + 1))
                tpose(128, (66, 194), rmaxb[:, 0:128], idn_b[:, :])
                tpose(128, (66, 194), rmaxb[:, 128:256], idn_b[:, :])
                tn.wait_ge(sem("shc"), b + 1)
                tpose(1, (194, 322), shb16[:, 0:1], idn_b[:, :])
                tpose(1, (194, 322), shb16[:, 1:2], idn_b[:, :])
                tn.wait_ge(sem("yqe"), 40 * (b + 1))
                if b >= 1:
                    tn.wait_ge(sem("amQ"), b)
                tn.matmul(pB[:, 0:256], E128[:, :], mvp_q[:, 0:256], start=True, stop=True).then_inc(sem("sqT"), 1)
                tn.wait_ge(sem("qa2"), 32 * (b + 1))
                for p2 in range(2):
                    gp_ = 2 * b + p2
                    if gp_ >= 1:
                        tn.wait_ge(sem("de1"), gp_)
                    for ng in range(2):
                        bank = pC if ng == 0 else pD
                        for k in range(32):
                            mm = tn.matmul(bank[:, :], hqT[:, k * 256 + p2 * 128:k * 256 + (p2 + 1) * 128],
                                           w2qT[:, k * 1024 + ng * 512:k * 1024 + (ng + 1) * 512],
                                           start=(k == 0), stop=(k == 31))
                    mm.then_inc(sem("dn"), 1)
            bar(tn, 4)

        @block.vector
        def _(ve):
            ve.memset(ones_col[:, :], 1.0)
            ve.memset(E128[:, :], 0.0)
            ve.memset(E128[0:1, :], 1.0)
            ve.memset(ones128[:, :], 1.0)
            ve.memset(lrp[:, :], 0.0)
            for wi in range(2):
                for pp in range(2):
                    ve.memset(accw[wi][pp][:, :], 0.0)
            for t in range(NW):
                wi, tl = (0, t) if t < 8 else (1, t - 8)
                if t % 2 == 0:
                    ve.wait_ge(sem("fw0E"), CUME[t])
                else:
                    ve.wait_ge(sem("fw0O"), CUMO[t])
                ve.tensor_reduce(t8a[:, 2:3], fwbuf[t % 2][:, :], AX.X, ALU.add, apply_absolute_value=True)
                gap(ve)
                ve.tensor_tensor(accw[wi][(tl + 1) % 2][:, 0:1], accw[wi][tl % 2][:, 0:1], t8a[:, 2:3], ALU.add).then_inc(sem("mn"), 2 if t == NW - 1 else 1)
            ve.wait_ge(sem("sw1"), 1)
            for wi in range(2):
                ve.tensor_scalar(t8b[:, 0:1], pS[:, wi:wi + 1], 1.0 / WCNT, None, ALU.mult)
                gap(ve)
                ve.tensor_scalar_max(mwc[wi][:, 0:1], t8b[:, 0:1], 1e-5)
                gap(ve)
                r = recip1(ve, swq[wi], mwc[wi][:, 0:1], t8a)
            r.then_inc(sem("swd"), 1)
            bar(ve, 1)
            for t in range(NW):
                if t % 2 == 0:
                    ve.wait_ge(sem("fwbE"), CUME[t])
                else:
                    ve.wait_ge(sem("fwbO"), CUMO[t])
                if t >= 1:
                    ve.wait_ge(sem("tcast"), 2 * t)
                if t >= 9:
                    ve.wait_ge(sem("w2st"), 16 * (t - 8))
                wi = 0 if t < 8 else 1
                ve.tensor_scalar(ftmp[:, :], fsb[t % 2][:, :], swq[wi][:, 0:1], -1.0, ALU.mult, ALU.max)
                ve.tensor_scalar(ftmp[:, :], ftmp[:, :], 1.0, None, ALU.min).then_inc(sem("tclip"), 1)
            bar(ve, 2)
            ve.memset(dbuf[0][:, :], 1.0)
            ve.memset(lnsml[:, :], 0.0)
            ve.memset(d3buf[0][:, :], 1.0).then_inc(sem("d"), 1)
            for i in range(NCH):
                ve.wait_ge(sem("uvl"), 16 * (i + 1))
                ve.wait_ge(sem("vvl"), 16 * (i + 1))
                for hf in range(2):
                    ve.wait_ge(sem("sigh"), 2 * i + hf + 1)
                    ve.tensor_tensor(tt2[:, :], ubuf[i % 3][:, hf * 1024:(hf + 1) * 1024],
                                     vbuf[i % 2][:, hf * 1024:(hf + 1) * 1024], ALU.mult)
                    ve.tensor_tensor(gbf[:, hf * 1024:(hf + 1) * 1024], tsig[:, :], tt2[:, :], ALU.mult).then_inc(sem("gh"), 1)
                ve.wait_ge(sem("ge1"), i + 1)
                ve.wait_ge(sem("xvl"), 16 * (i + 1))
                ve.tensor_sub(tdf[:, :], g1t[:, :], xnb[i % 2][:, :]).then_inc(sem("df"), 1)
                ve.tensor_add(trs[:, :], g1t[:, :], xnb[i % 2][:, :]).then_inc(sem("rs"), 1)
                ve.tensor_reduce(lnsml[0:64, 2:3], trs[:, :], AX.X, ALU.add)
                gap(ve)
                ve.tensor_scalar(lnsml[0:64, 3:4], lnsml[0:64, 2:3], 1.0 / H, None, ALU.mult)
                gap(ve)
                ve.tensor_tensor(lnsml[0:64, 4:5], lnsml[0:64, 3:4], lnsml[0:64, 3:4], ALU.mult)
                ve.wait_ge(sem("ln1"), i + 1)
                ve.tensor_scalar(lnsml[0:64, 5:6], lnsml[0:64, 1:2], 1.0 / H, EPS, ALU.mult, ALU.add)
                gap(ve)
                ve.tensor_tensor(lnsml[0:64, 6:7], lnsml[0:64, 5:6], lnsml[0:64, 4:5], ALU.subtract).then_inc(sem("ln2"), 1)
                ve.wait_ge(sem("ln3"), i + 1)
                recip1(ve, t8b, lnsml[0:64, 7:8], t8a, p=64)
                gap(ve)
                if i >= 2:
                    ve.wait_ge(sem("mail"), 32 * (i - 1))
                    ve.wait_ge(sem("yst"), 16 * (i - 1))
                ve.tensor_scalar(ybuf[i % 2][:, :], trs[:, :], lnsml[0:64, 3:4], t8b[0:64, 0:1],
                                 ALU.subtract, ALU.mult).then_inc(sem("yy"), 1)
                ve.wait_ge(sem("tm"), i + 1)
                ve.tensor_copy(lsb[i % 2][0:1, 0:1], pS[0:1, 0:1]).then_inc(sem("lse"), 1)
                ve.wait_ge(sem("bc"), i + 1)
                ve.tensor_reduce(t8a[:, 0:1], pC[:, 0:8], AX.X, ALU.add)
                gap(ve)
                ve.tensor_scalar(t8b[:, 1:2], t8a[:, 0:1], -C_LOSS, 1.0, ALU.mult, ALU.add)
                gap(ve)
                ve.tensor_tensor(dbuf[(i + 1) % 2][:, 0:1], dbuf[i % 2][:, 0:1], t8b[:, 1:2], ALU.mult)
                gap(ve)
                ve.tensor_tensor(t8a[:, 1:2], dbuf[(i + 1) % 2][:, 0:1], dbuf[(i + 1) % 2][:, 0:1], ALU.mult)
                gap(ve)
                ve.tensor_tensor(d3buf[(i + 1) % 2][:, 0:1], t8a[:, 1:2], dbuf[(i + 1) % 2][:, 0:1], ALU.mult).then_inc(sem("d"), 1)
            bar(ve, 3)
            ve.memset(yAB[0][0:1, :], 0.0)
            ve.memset(mvp_s[:, :], 0.0)
            ve.memset(mvp_q[:, :], 0.0).then_inc(sem("z0v"), 1)
            ve.wait_ge(sem("ld0"), 16 * 4)
            ve.tensor_scalar(t8a[:, 3:4], mwc[0][:, 0:1], 1.0 / 127.0, None, ALU.mult)
            gap(ve)
            ve.tensor_scalar_mul(cw_s[:, :], cw_t[:, :], t8a[:, 3:4]).then_inc(sem("cws"), 1)
            TPN = [0]
            for b in range(8):
                ve.memset(rmax[:, :], 0.0)
                ve.memset(rmin[:, :], 0.0)
                if b >= 1:
                    ve.wait_ge(sem("up"), 32 * b)
                    ve.wait_ge(sem("sbT"), b)
                    ve.wait_ge(sem("sqT"), b)
                for j in range(4):
                    c = 4 * b + j
                    ve.wait_ge(sem("yl"), 16 * (c + 1))
                    ve.wait_ge(sem("hg"), 16 * (c + 1))
                    ve.tensor_reduce(ayt[0:66, 0:1], yt[c % 2][:, :], AX.X, ALU.max, apply_absolute_value=True)
                    gap(ve)
                    ve.tensor_scalar_max(ayt[0:66, 1:2], ayt[0:66, 0:1], 1e-5)
                    gap(ve)
                    ve.tensor_copy(aycb[0:66, 0:1], ayt[0:66, 1:2])
                    gap(ve)
                    ve.tensor_copy(ayf[0:66, 0:1], aycb[0:66, 0:1])
                    gap(ve)
                    ve.tensor_scalar_mul(t66a[0:66, 0:8], ones_col[0:66, 0:8], ayf[0:66, 0:1])
                    gap(ve)
                    ve.reciprocal(t66b[0:66, 0:8], t66a[0:66, 0:8])
                    gap(ve)
                    ve.tensor_scalar(sy66[0:66, 0:1], t66b[0:66, 0:1], 127.0, None, ALU.mult).then_inc(sem("q1"), 1)
                    for k in range(8):
                        T = TPN[0]
                        ve.wait_ge(sem("tpq"), T + 1)
                        dstv = yqT.rearrange("p (kk x) -> p kk x", kk=8)[:, k, j * 66:(j + 1) * 66]
                        ve.tensor_copy(dstv, TPB[T % 2][:, 0:66]).then_inc(sem("yqe"), 1)
                        TPN[0] += 1
                    T = TPN[0]
                    ve.wait_ge(sem("tpq"), T + 1)
                    ve.tensor_copy(mvp_s[0:1, j * 66:(j + 1) * 66], TPB[T % 2][0:1, 0:66]).then_inc(sem("yqe"), 1)
                    TPN[0] += 1
                ve.wait_ge(sem("sbT"), b + 1)
                ve.tensor_copy(sinv_bc[:, :], pA[:, 0:264]).then_inc(sem("sb"), 1)
                ve.wait_ge(sem("cws"), 1)
                for m in range(32):
                    ve.wait_ge(sem("up"), 32 * b + m + 1)
                    bank = pE if m % 2 == 0 else pF
                    ve.tensor_tensor(htmp[m % 2][:, :], bank[:, 0:264], sinv_bc[:, :], ALU.mult).then_inc(sem("hT"), 1)
                    ve.wait_ge(sem("cva"), 32 * b + m + 1)
                    wa = cw_s[:, 3 * m + 0:3 * m + 1]
                    wc = cw_s[:, 3 * m + 2:3 * m + 3]
                    h4 = htmp[m % 2].rearrange("p (g x) -> p g x", g=4)
                    A4 = cvA.rearrange("p (g x) -> p g x", g=4)
                    B4 = cvB.rearrange("p (g x) -> p g x", g=4)
                    C4 = cvC.rearrange("p (g x) -> p g x", g=4)
                    ve.scalar_tensor_tensor(B4[:, :, 2:3], h4[:, :, 0:1], wa, A4[:, :, 2:3], ALU.mult, ALU.add)
                    ve.scalar_tensor_tensor(B4[:, :, 3:66], h4[:, :, 2:65], wa, A4[:, :, 3:66], ALU.mult, ALU.add)
                    ve.scalar_tensor_tensor(C4[:, :, 2:65], h4[:, :, 3:66], wc, B4[:, :, 2:65], ALU.mult, ALU.add)
                    ve.scalar_tensor_tensor(C4[:, :, 65:66], h4[:, :, 1:2], wc, B4[:, :, 65:66], ALU.mult, ALU.add).then_inc(sem("cvd"), 1)
                    ve.wait_ge(sem("sil"), 32 * b + m + 1)
                    ve.tensor_tensor(rmax[:, :], rmax[:, :], hsb[:, m * 256:(m + 1) * 256], ALU.max)
                    ve.tensor_tensor(rmin[:, :], rmin[:, :], hsb[:, m * 256:(m + 1) * 256], ALU.min).then_inc(sem("cvv"), 1)
                ve.tensor_scalar(rmin[:, :], rmin[:, :], -1.0, None, ALU.mult)
                ve.tensor_tensor(rmax[:, :], rmax[:, :], rmin[:, :], ALU.max)
                ve.tensor_copy(rmaxb[:, :], rmax[:, :]).then_inc(sem("rmb"), 1)
                for q in range(2):
                    T = TPN[0]
                    ve.wait_ge(sem("tpq"), T + 1)
                    ve.tensor_reduce(amaxh[:, q:q + 1], TPB[T % 2][:, 66:194], AX.X, ALU.max).then_inc(sem("yqe"), 1)
                    TPN[0] += 1
                gap(ve)
                ve.tensor_scalar_max(amaxh[:, 2:4], amaxh[:, 0:2], 1e-5)
                gap(ve)
                for q in range(2):
                    recip1(ve, t8b, amaxh[:, 2 + q:3 + q], t8a)
                    gap(ve)
                    ve.tensor_scalar(t8a[:, 4:5], t8b[:, 0:1], 127.0, None, ALU.mult)
                    gap(ve)
                    ve.tensor_copy(shb16[:, q:q + 1], t8a[:, 4:5])
                    gap(ve)
                ve.tensor_copy(shbf[:, 0:2], shb16[:, 0:2])
                gap(ve)
                for q in range(2):
                    recip1(ve, t8b, shbf[:, q:q + 1], t8a)
                    gap(ve)
                    r = ve.tensor_scalar_mul(shv[:, q:q + 1], t8b[:, 0:1], mwc[1][:, 0:1])
                    gap(ve)
                r.then_inc(sem("shc"), 1)
                for q in range(2):
                    T = TPN[0]
                    ve.wait_ge(sem("tpq"), T + 1)
                    ve.tensor_copy(mvp_q[0:1, q * 128:(q + 1) * 128], TPB[T % 2][0:1, 194:322]).then_inc(sem("yqe"), 1)
                    TPN[0] += 1
                ve.wait_ge(sem("sqT"), b + 1)
                ve.tensor_copy(sq_bc[:, :], pB[:, 0:256]).then_inc(sem("amQ"), 1)
                for m in range(32):
                    if m >= 1:
                        ve.wait_ge(sem("qa"), 32 * b + m)
                    ve.tensor_tensor(qtmp, hsb[:, m * 256:(m + 1) * 256], sq_bc[:, :], ALU.mult).then_inc(sem("qv"), 1)
                for p2 in range(2):
                    gp_ = 2 * b + p2
                    ve.wait_ge(sem("de1"), gp_ + 1)
                    ve.wait_ge(sem("yab"), 16 * (gp_ + 1))
                    ve.tensor_add(outt[:, :], outt[:, :], yAB[gp_ % 2][:, :]).then_inc(sem("rv0"), 1)
                    ve.tensor_reduce(lnsml2[:, 2:3], outt[:, :], AX.X, ALU.add)
                    gap(ve)
                    ve.tensor_scalar(lnsml2[:, 3:4], lnsml2[:, 2:3], 1.0 / H, None, ALU.mult)
                    gap(ve)
                    ve.tensor_tensor(lnsml2[:, 4:5], lnsml2[:, 3:4], lnsml2[:, 3:4], ALU.mult)
                    ve.wait_ge(sem("ra1"), gp_ + 1)
                    ve.tensor_scalar(lnsml2[:, 5:6], lnsml2[:, 1:2], 1.0 / H, EPS, ALU.mult, ALU.add)
                    gap(ve)
                    ve.tensor_tensor(lnsml2[:, 6:7], lnsml2[:, 5:6], lnsml2[:, 4:5], ALU.subtract).then_inc(sem("rv2"), 1)
                    ve.wait_ge(sem("ra2"), gp_ + 1)
                    recip1(ve, t8b, lnsml2[:, 7:8], t8a)
                    gap(ve)
                    ve.tensor_scalar(yAB[gp_ % 2][:, :], outt[:, :], lnsml2[:, 3:4], t8b[:, 0:1],
                                     ALU.subtract, ALU.mult).then_inc(sem("ot"), 1)
            bar(ve, 4)

        @block.scalar
        def _(sc):
            for pr in fw_tiles(1, fwbuf[1]):
                sc.dma_start(out=pr[0], in_=pr[1]).then_inc(sem("fw0O"), 16)
            n = 0
            for g in range(4):
                for uvi in range(2):
                    for m in range(32):
                        if n % 16 == 0 and 2 + n // 16 < NW and (2 + n // 16) % 2 == 1:
                            t = 2 + n // 16
                            sc.wait_ge(sem("mn"), t - 1)
                            for pr in fw_tiles(t, fwbuf[t % 2]):
                                sc.dma_start(out=pr[0], in_=pr[1]).then_inc(sem("fw0O"), 16)
                        sc.wait_ge(sem("mm0"), n + 1)
                        if n >= 2:
                            if n % 2 == 0:
                                sc.wait_ge(sem("st0a"), 16 * (n // 2))
                            else:
                                sc.wait_ge(sem("st0b"), 16 * (n // 2))
                        bank = pA if n % 2 == 0 else pB
                        sc.copy(uvtmp[n % 2][:, :], bank[:, :]).then_inc(sem("ev0"), 1)
                        if n % 2 == 1:
                            src = uvtmp[1].rearrange("p (c t) -> p c t", c=8)
                            dst = uv_hbm[uvi, 8 * g:8 * g + 8, :, m * 64:(m + 1) * 64].rearrange("c p t -> p c t")
                            sc.dma_start(out=dst, in_=src).then_inc(sem("st0b"), 16)
                        n += 1
            bar(sc, 1)
            for t in range(NW):
                sc.wait_ge(sem("tclip"), t + 1)
                if t + 2 < NW and t % 2 == 0:
                    for pr in fw_tiles(t + 2, fsb[t % 2]):
                        sc.dma_start(out=pr[0], in_=pr[1]).then_inc(sem("fwbE"), 16)
                sc.copy(itmp[:, :], ftmp[:, :]).then_inc(sem("tcast"), 1)
                if t < 8:
                    sc.copy(w1qT[:, t * FH:(t + 1) * FH], itmp[:, :]).then_inc(sem("tcast"), 1)
                else:
                    if t > 8:
                        sc.wait_ge(sem("w2st"), 16 * (t - 8))
                    sc.copy(bfv[:, :], itmp[:, :]).then_inc(sem("tcast"), 1)
            bar(sc, 2)
            for j in range(2):
                sc.dma_start(out=vbuf[j][:, :], in_=uv_hbm[1, j, :, :]).then_inc(sem("vvl"), 16)
            for i in range(NCH):
                sc.wait_ge(sem("d"), i + 1)
                sc.wait_ge(sem("uvl"), 16 * (i + 1))
                for hf in range(2):
                    if 2 * i + hf >= 1:
                        sc.wait_ge(sem("gh"), 2 * i + hf)
                    sc.activation(tsig[:, :], ubuf[i % 3][:, hf * 1024:(hf + 1) * 1024], AF.Sigmoid,
                                  scale=dbuf[i % 2][:, 0:1]).then_inc(sem("sigh"), 1)
                sc.wait_ge(sem("mm1"), i + 1)
                d3ap = d3buf[i % 2][0:64, 0:1]
                sc.activation(g1t[:, 0:512], pA[0:64, :], AF.Identity, scale=d3ap)
                sc.activation(g1t[:, 512:1024], pB[0:64, :], AF.Identity, scale=d3ap).then_inc(sem("ge1"), 1)
                if i + 2 < NCH:
                    sc.dma_start(out=vbuf[(i + 2) % 2][:, :], in_=uv_hbm[1, i + 2, :, :]).then_inc(sem("vvl"), 16)
                sc.wait_ge(sem("df"), i + 1)
                sc.activation(gbf[0:64, 0:1024], tdf[:, :], AF.Square, accum_out=lnsml[0:64, 0:1]).then_inc(sem("lq"), 1)
                sc.wait_ge(sem("rs"), i + 1)
                sc.activation(gbf[0:64, 0:1024], trs[:, :], AF.Square, accum_out=lnsml[0:64, 1:2]).then_inc(sem("ln1"), 1)
                sc.wait_ge(sem("ln2"), i + 1)
                sc.activation(lnsml[0:64, 7:8], lnsml[0:64, 6:7], AF.Sqrt).then_inc(sem("ln3"), 1)
            bar(sc, 3)
            for b in range(8):
                for j in range(4):
                    c = 4 * b + j
                    sc.wait_ge(sem("q1"), c + 1)
                    if c >= 1:
                        bb, jj = (c - 1) // 4, (c - 1) % 4
                        sc.wait_ge(sem("tpq"), 40 * bb + 9 * (jj + 1))
                    for hf in range(2):
                        sc.activation(yqi[:, :], yt[c % 2][:, hf * 512:(hf + 1) * 512], AF.Identity,
                                      scale=sy66[0:66, 0:1])
                        a2 = sc.activation(yqb[:, hf * 512:(hf + 1) * 512], yqi[:, :], AF.Identity)
                    a2.then_inc(sem("q2"), 1)
                for m in range(32):
                    sc.wait_ge(sem("hT"), 32 * b + m + 1)
                    if m >= 1:
                        sc.wait_ge(sem("cvd"), 32 * b + m)
                    sc.activation(cvA[:, :], htmp[m % 2][:, :], AF.Identity,
                                  scale=cw_s[:, 3 * m + 1:3 * m + 2], bias=cb_t[:, m:m + 1]).then_inc(sem("cva"), 1)
                    sc.wait_ge(sem("cvd"), 32 * b + m + 1)
                    c4 = cvC.rearrange("p (g x) -> p g x", g=4)[:, :, 2:66]
                    s4 = hsb[:, m * 256:(m + 1) * 256].rearrange("p (g x) -> p g x", g=4)
                    sc.activation(s4, c4, AF.Silu).then_inc(sem("sil"), 1)
                if b >= 1:
                    sc.wait_ge(sem("dn"), 2 * b)
                for m in range(32):
                    sc.wait_ge(sem("qv"), 32 * b + m + 1)
                    sc.copy(hint[:, :], qtmp).then_inc(sem("qa"), 1)
                    sc.copy(hqT[:, m * 256:(m + 1) * 256], hint[:, :]).then_inc(sem("qa2"), 1)
                for p2 in range(2):
                    gp_ = 2 * b + p2
                    sc.wait_ge(sem("dn"), gp_ + 1)
                    if gp_ >= 1:
                        sc.wait_ge(sem("ot"), gp_)
                    shp = shv[:, p2:p2 + 1]
                    sc.activation(outt[:, 0:512], pC[:, :], AF.Identity, scale=shp)
                    sc.activation(outt[:, 512:1024], pD[:, :], AF.Identity, scale=shp).then_inc(sem("de1"), 1)
                    sc.wait_ge(sem("rv0"), gp_ + 1)
                    sc.activation(yAB[gp_ % 2][:, :], outt[:, :], AF.Square, accum_out=lnsml2[:, 1:2]).then_inc(sem("ra1"), 1)
                    sc.wait_ge(sem("rv2"), gp_ + 1)
                    sc.activation(lnsml2[:, 7:8], lnsml2[:, 6:7], AF.Sqrt).then_inc(sem("ra2"), 1)
            bar(sc, 4)

    return nc


def kernel(L0_attn_x, L0_attn_w0, L0_attn_w1, L0_attn_w2,
           L0_ffn_w1, L0_ffn_conv_w, L0_ffn_conv_b, L0_ffn_w2):
    x = np.ascontiguousarray(np.asarray(L0_attn_x, np.float32)[0])
    w0 = np.ascontiguousarray(np.asarray(L0_attn_w0, np.float32))
    w1 = np.ascontiguousarray(np.asarray(L0_attn_w1, np.float32))
    w2 = np.ascontiguousarray(np.asarray(L0_attn_w2, np.float32))
    fw1 = np.asarray(L0_ffn_w1, np.float32)
    fw2 = np.asarray(L0_ffn_w2, np.float32)
    cw = np.asarray(L0_ffn_conv_w, np.float32)
    cb = np.asarray(L0_ffn_conv_b, np.float32)

    fw1T = np.ascontiguousarray(fw1.T)
    fw2T = np.ascontiguousarray(fw2.T)
    cwv = np.ascontiguousarray(cw[:, 0, :].reshape(32, 128, 3).transpose(1, 0, 2).reshape(128, 96))
    cbv = np.ascontiguousarray(cb.reshape(32, 128).T)
    idnb = np.eye(128, dtype=ml_dtypes.bfloat16)

    nc = build_nc()
    in_maps = []
    tok_rows = []
    for c in range(8):
        rows = (np.arange(NCH)[:, None] * CHT + 64 * c + np.arange(64)[None, :]).reshape(-1)
        tok_rows.append(rows)
        xn = np.ascontiguousarray(x[rows])
        xt = np.ascontiguousarray(xn.T)
        hidx = np.zeros((2, NCH), np.uint32)
        for j in range(NCH):
            if c > 0:
                hidx[0, j] = ((c - 1) * 33 + (j + 1)) * 2 + 1
            elif j > 0:
                hidx[0, j] = (7 * 33 + j) * 2 + 1
            else:
                hidx[0, j] = ZROW
            if c < 7:
                hidx[1, j] = ((c + 1) * 33 + (j + 1)) * 2 + 0
            elif j < NCH - 1:
                hidx[1, j] = (0 * 33 + (j + 2)) * 2 + 0
            else:
                hidx[1, j] = ZROW
        in_maps.append({
            "xn": xn, "xt": xt, "w0": w0, "w2": w2, "w1": w1,
            "fw1T": fw1T, "fw2T": fw2T, "cwv": cwv, "cbv": cbv,
            "idnb": idnb, "hidx": hidx,
        })

    res = run_bass_kernel_spmd(nc, in_maps, core_ids=list(range(8)))
    out = np.empty((S, H), np.float32)
    for c in range(8):
        out[tok_rows[c]] = res.results[c]["o"]
    return out[None]


# version 08 (final)
# speedup vs baseline: 1.2775x; 1.2106x over previous
import numpy as np
import ml_dtypes
from contextlib import ExitStack

import concourse.bass as bass
import concourse.mybir as mybir
from concourse.bass_utils import run_bass_kernel_spmd
from concourse.bass_core_rust import IndirectOffsetOnAxis

F32 = mybir.dt.float32
BF16 = mybir.dt.bfloat16
I32 = mybir.dt.int32
U32 = mybir.dt.uint32
AF = mybir.ActivationFunctionType
ALU = mybir.AluOpType
AX = mybir.AxisListType

S, H, FH = 16384, 1024, 4096
NCH, CHT, TPC = 32, 512, 64
NTOK = NCH * TPC
LR = 1e-3
EPS = 1e-5
RG = [[0, 1, 2, 3, 4, 5, 6, 7]]
SLAB = 3072
ZROW = 528
C_LOSS = LR / (CHT * H)
WCNT = float(H * FH)
NW = 16


def build_nc():
    nc = bass.Bass("TRN2", target_bir_lowering=False)

    xn_d = nc.dram_tensor("xn", [NTOK, H], F32, kind="ExternalInput").ap()
    xt_d = nc.dram_tensor("xt", [H, NTOK], F32, kind="ExternalInput").ap()
    w0_d = nc.dram_tensor("w0", [H, FH], F32, kind="ExternalInput").ap()
    w2_d = nc.dram_tensor("w2", [H, FH], F32, kind="ExternalInput").ap()
    w1_d = nc.dram_tensor("w1", [FH, H], F32, kind="ExternalInput").ap()
    fw1T_d = nc.dram_tensor("fw1T", [H, FH], F32, kind="ExternalInput").ap()
    fw2T_d = nc.dram_tensor("fw2T", [FH, H], F32, kind="ExternalInput").ap()
    cwv_d = nc.dram_tensor("cwv", [128, 96], F32, kind="ExternalInput").ap()
    cbv_d = nc.dram_tensor("cbv", [128, 32], F32, kind="ExternalInput").ap()
    idnb_d = nc.dram_tensor("idnb", [128, 128], BF16, kind="ExternalInput").ap()
    hidx_d = nc.dram_tensor("hidx", [2, NCH], U32, kind="ExternalInput").ap()
    o_d = nc.dram_tensor("o", [NTOK, H], F32, kind="ExternalOutput").ap()

    uv_hbm = nc.dram_tensor("uv_hbm", [2, NCH, 128, NTOK], BF16, kind="Internal").ap()
    y_hbm = nc.dram_tensor("y_hbm", [NCH, TPC, H], F32, kind="Internal").ap()
    w2q_hbm = nc.dram_tensor("w2q_hbm", [128, 32768], BF16, kind="Internal").ap()
    agy_in = nc.dram_tensor("agy_in", [33, 2048], F32, kind="Internal").ap()
    agl_in = nc.dram_tensor("agl_in", [33, 1], F32, kind="Internal").ap()
    lossag = nc.dram_tensor("lossag", [33, 8], F32, kind="Internal",
                            addr_space="Shared").ap()
    hal_all = nc.dram_tensor("hal_all", [529, 1024], F32, kind="Internal",
                             addr_space="Shared").ap()

    ctx = ExitStack()
    with ctx:
        e = ctx.enter_context
        SB_BYTES = 212736
        arena_t = e(nc.sbuf_tensor("arena", [128, SB_BYTES], mybir.dt.uint8))
        hmap = {F32: arena_t.bitcast(F32), BF16: arena_t.bitcast(BF16),
                I32: arena_t.bitcast(I32), U32: arena_t.bitcast(U32)}
        esz = {F32: 4, BF16: 2, I32: 4, U32: 4}
        cur = [0]

        def take(dt, p, n):
            off = (cur[0] + 63) // 64 * 64
            cur[0] = off + n * esz[dt]
            assert cur[0] <= SB_BYTES, f"arena overflow {cur[0]}"
            return hmap[dt][0:p, off // esz[dt]: off // esz[dt] + n]

        def at(dt, off_b, p, n):
            assert off_b % 64 == 0
            return hmap[dt][0:p, off_b // esz[dt]: off_b // esz[dt] + n]

        ones_col = take(F32, 128, 8)
        E128 = take(F32, 128, 128)
        ones128 = take(F32, 128, 128)
        idn_b = take(BF16, 128, 128)
        cw_t = take(F32, 128, 96)
        cb_t = take(F32, 128, 32)
        dbuf = [take(F32, 128, 8) for _ in range(2)]
        d3buf = [take(F32, 128, 8) for _ in range(2)]
        accw = [[take(F32, 128, 8) for _ in range(2)] for _ in range(2)]
        mwc = [take(F32, 128, 8) for _ in range(2)]
        swq = [take(F32, 128, 8) for _ in range(2)]
        t8a = take(F32, 128, 8)
        t8b = take(F32, 128, 8)
        lsb = [take(F32, 1, 8) for _ in range(2)]
        lrp = take(F32, 128, 8)
        pad8 = take(F32, 128, 8)
        sidx = take(U32, 2, NCH)
        PM = (cur[0] + 63) // 64 * 64

        cur[0] = PM
        w0bf = take(BF16, 128, 8 * FH)
        w2bf = take(BF16, 128, 8 * FH)
        xtbf = take(BF16, 128, 8 * NTOK)
        uvtmp = [take(BF16, 128, 512) for _ in range(2)]
        fwbuf = [take(F32, 128, FH) for _ in range(2)]
        assert cur[0] <= SB_BYTES, f"P0 {cur[0]}"

        REGA = PM
        W1QT = PM + 65536
        POST = PM + 131072
        w1bf = at(BF16, REGA, 128, 32 * 1024)
        w2qT = at(BF16, REGA, 128, 32 * 1024)
        w1qT = at(BF16, W1QT, 128, 8 * FH)

        cur[0] = POST
        fsb = [take(F32, 128, FH) for _ in range(2)]
        ftmp = take(F32, 128, FH)
        itmp = take(I32, 128, FH)
        FSB_OFF = (POST + 63) // 64 * 64
        FT_OFF = (FSB_OFF + 2 * FH * 4 + 63) // 64 * 64
        bfv = at(BF16, FT_OFF, 128, FH)
        assert cur[0] <= SB_BYTES, f"P0b {cur[0]}"

        cur[0] = POST
        ubuf = [take(BF16, 128, NTOK) for _ in range(3)]
        vbuf = [take(BF16, 128, NTOK) for _ in range(2)]
        tsig = take(F32, 128, 1024)
        uvp = [take(F32, 128, NTOK) for _ in range(2)]
        gbf = take(BF16, 128, NTOK)
        xnb = [take(F32, 64, H) for _ in range(2)]
        g1t = take(F32, 64, H)
        tdf = tsig[0:64, :]
        trs = take(F32, 64, H)
        ybuf = [take(F32, 64, H) for _ in range(2)]
        lnsml = take(F32, 128, 16)
        assert cur[0] <= SB_BYTES, f"P1 {cur[0]}"

        cur[0] = POST
        yt = [take(F32, 66, H) for _ in range(2)]
        yqi = take(I32, 66, 512)
        yqb = take(BF16, 66, H)
        yqT = take(BF16, 128, 8 * 264)
        yAB = [take(F32, 128, H) for _ in range(2)]
        hsb = take(BF16, 128, 32 * 256)
        hqT = take(BF16, 128, 32 * 256)
        htmp = [take(F32, 128, 264) for _ in range(2)]
        cvA = take(F32, 128, 264)
        cvB = take(F32, 128, 264)
        cvC = take(F32, 128, 264)
        sinv_bc = take(F32, 128, 264)
        sq_bc = take(F32, 128, 256)
        hint = take(I32, 128, 256)
        rmax = take(F32, 128, 256)
        rmin = take(F32, 128, 256)
        rmaxb = take(BF16, 128, 256)
        ayt = take(F32, 66, 8)
        aycb = take(BF16, 66, 8)
        ayf = take(F32, 66, 8)
        sy66 = take(F32, 66, 8)
        t66a = take(F32, 66, 8)
        t66b = take(F32, 66, 8)
        mvp_s = take(F32, 128, 264)
        mvp_q = take(F32, 128, 256)
        amaxh = take(F32, 128, 8)
        shb16 = take(BF16, 128, 8)
        shbf = take(F32, 128, 8)
        shv = take(F32, 128, 8)
        cw_s = take(F32, 128, 96)
        outt = take(F32, 128, H)
        lnsml2 = take(F32, 128, 16)
        assert cur[0] <= SB_BYTES, f"P2 {cur[0]}"
        qtmp = cvA[:, 0:256]

        pA = e(nc.psum_tensor("pA", [128, 512], F32))
        pB = e(nc.psum_tensor("pB", [128, 512], F32))
        pC = e(nc.psum_tensor("pC", [128, 512], F32))
        pD = e(nc.psum_tensor("pD", [128, 512], F32))
        pE = e(nc.psum_tensor("pE", [128, 512], F32))
        pF = e(nc.psum_tensor("pF", [128, 512], F32))
        pS = e(nc.psum_tensor("pS", [128, 512], F32))
        pT = e(nc.psum_tensor("pT", [128, 1024], BF16))
        TPB = [pT[:, 0:1024], pS.bitcast(BF16)[:, 0:1024]]

        sems = {}

        def sem(name):
            if name not in sems:
                sems[name] = e(nc.semaphore(name))
            return sems[name]

        ph = sem("ph")

        def bar(eng, k, drain=True):
            if drain:
                eng.drain()
            eng.nop().then_inc(ph, 1)
            eng.wait_ge(ph, 5 * k)

        def gap(vec):
            vec.tensor_tensor(pad8[:, 0:1], pad8[:, 0:1], ones_col[:, 0:1], ALU.add)

        def recip1(vec, out8, in1ap, tmp8, p=128):
            vec.tensor_scalar_mul(tmp8[0:p, 0:8], ones_col[0:p, 0:8], in1ap)
            gap(vec)
            return vec.reciprocal(out8[0:p, 0:8], tmp8[0:p, 0:8])

        def fw_tiles(t, buf):
            if t < 8:
                return [(buf[:, :], fw1T_d[t * 128:(t + 1) * 128, :])]
            k4 = t - 8
            return [(buf[:, a * 1024:(a + 1) * 1024],
                     fw2T_d[k4 * 512 + a * 128:k4 * 512 + (a + 1) * 128, :]) for a in range(4)]

        FWCNT = [16 if _t < 8 else 64 for _t in range(NW)]
        def parity_cum(par):
            out = {}
            c = 0
            for _t in range(par, NW, 2):
                c += FWCNT[_t]
                out[_t] = c
            return out
        CUME, CUMO = parity_cum(0), parity_cum(1)

        block = e(nc.Block())

        @block.sync
        def _(sy):
            sy.dma_start(out=cw_t[:, :], in_=cwv_d[:, :]).then_inc(sem("ld0"), 16)
            sy.dma_start(out=cb_t[:, :], in_=cbv_d[:, :]).then_inc(sem("ld0"), 16)
            sy.dma_start(out=idn_b[:, :], in_=idnb_d[:, :]).then_inc(sem("ld0"), 16)
            sy.dma_start(out=sidx[:, :], in_=hidx_d[:, :]).then_inc(sem("ld0"), 16)
            for pr in fw_tiles(0, fwbuf[0]):
                sy.dma_start(out=pr[0], in_=pr[1]).then_inc(sem("fw0E"), 16)
            for n in range(0, 256, 2):
                g, uvi, m = n // 64, (n // 32) % 2, n % 32
                if n % 16 == 0 and 2 + n // 16 < NW and (2 + n // 16) % 2 == 0:
                    t = 2 + n // 16
                    sy.wait_ge(sem("mn"), t - 1)
                    for pr in fw_tiles(t, fwbuf[t % 2]):
                        sy.dma_start(out=pr[0], in_=pr[1]).then_inc(sem("fw0E"), 16)
                sy.wait_ge(sem("ev0"), n + 1)
                src = uvtmp[0].rearrange("p (c t) -> p c t", c=8)
                dst = uv_hbm[uvi, 8 * g:8 * g + 8, :, m * 64:(m + 1) * 64].rearrange("c p t -> p c t")
                sy.dma_start(out=dst, in_=src).then_inc(sem("st0a"), 16)
            sy.wait_ge(sem("st0a"), 16 * 128)
            sy.wait_ge(sem("st0b"), 16 * 128)
            bar(sy, 1, drain=False)
            for t0 in (0, 1):
                for pr in fw_tiles(t0, fsb[t0 % 2]):
                    sy.dma_start(out=pr[0], in_=pr[1]).then_inc(sem("fwbE") if t0 % 2 == 0 else sem("fwbO"), 16)
            for t in range(3, NW, 2):
                sy.wait_ge(sem("tclip"), t - 1)
                for pr in fw_tiles(t, fsb[t % 2]):
                    sy.dma_start(out=pr[0], in_=pr[1]).then_inc(sem("fwbO"), 16)
            sy.wait_ge(sem("w2st"), 16 * 8)
            bar(sy, 2, drain=False)
            def uld(i):
                sy.dma_start(out=ubuf[i % 3][:, :], in_=uv_hbm[0, i, :, :]).then_inc(sem("uvl"), 16)
            def xld(i):
                sy.dma_start(out=xnb[i % 2][:, :], in_=xn_d[i * 64:(i + 1) * 64, :]).then_inc(sem("xvl"), 16)
            for j in range(3):
                uld(j)
            for j in range(2):
                xld(j)
            for i in range(NCH):
                if i + 2 < NCH:
                    sy.wait_ge(sem("rs"), i + 1)
                    xld(i + 2)
                if i + 3 < NCH:
                    sy.wait_ge(sem("gh"), 2 * (i + 1))
                    uld(i + 3)
                sy.wait_ge(sem("yy"), i + 1)
                sy.dma_start(out=y_hbm[i, :, :], in_=ybuf[i % 2][:, :]).then_inc(sem("yst"), 16)
                sy.dma_start(out=agy_in[i + 1:i + 2, 0:1024], in_=ybuf[i % 2][0:1, :]).then_inc(sem("mail"), 16)
                sy.dma_start(out=agy_in[i + 1:i + 2, 1024:2048], in_=ybuf[i % 2][63:64, :]).then_inc(sem("mail"), 16)
                sy.wait_ge(sem("agd"), i + 1)
                if i >= 1:
                    sy.wait_ge(sem("bc"), i)
                sy.dma_start(out=lrp[0:1, 0:8], in_=lossag[i:i + 1, 0:8]).then_inc(sem("lrq"), 16)
            sy.wait_ge(sem("yst"), 16 * NCH)
            bar(sy, 3, drain=False)
            for q in range(4):
                sy.dma_start(out=w2qT[:, q * 8192:(q + 1) * 8192], in_=w2q_hbm[:, q * 8192:(q + 1) * 8192]).then_inc(sem("w2l"), 16)
            sy.wait_ge(sem("z0v"), 1)
            sy.dma_start(out=hal_all[528:529, :], in_=yAB[0][0:1, :]).then_inc(sem("z0"), 16)
            for b in range(8):
                for j in range(4):
                    c = 4 * b + j
                    if c >= 2:
                        sy.wait_ge(sem("q2"), c - 1)
                    sy.dma_start(out=yt[c % 2][2:66, :], in_=y_hbm[c, :, :]).then_inc(sem("yl"), 16)
                for p2 in range(2):
                    gp_ = 2 * b + p2
                    cA = 4 * b + 2 * p2
                    if gp_ >= 2:
                        sy.wait_ge(sem("ost"), 16 * (gp_ - 1))
                    sy.dma_start(out=yAB[gp_ % 2][:, :], in_=y_hbm[cA:cA + 2, :, :].rearrange("a b c -> (a b) c")).then_inc(sem("yab"), 16)
                    sy.wait_ge(sem("ot"), gp_ + 1)
                    sy.dma_start(out=o_d[64 * cA:64 * cA + 128, :], in_=yAB[gp_ % 2][:, :]).then_inc(sem("ost"), 16)
            sy.wait_ge(sem("ost"), 16 * 16)
            bar(sy, 4, drain=False)

        @block.gpsimd
        def _(gp):
            for k in range(8):
                gp.dma_start(out=xtbf[:, k * NTOK:(k + 1) * NTOK], in_=xt_d[k * 128:(k + 1) * 128, :]).then_inc(sem("ldw"), 16)
            for k in range(8):
                gp.dma_start(out=w0bf[:, k * FH:(k + 1) * FH], in_=w0_d[k * 128:(k + 1) * 128, :]).then_inc(sem("ldw"), 16)
            for k in range(8):
                gp.dma_start(out=w2bf[:, k * FH:(k + 1) * FH], in_=w2_d[k * 128:(k + 1) * 128, :]).then_inc(sem("ldw"), 16)
            bar(gp, 1)
            for k in range(32):
                gp.dma_start(out=w1bf[:, k * 1024:(k + 1) * 1024], in_=w1_d[k * 128:(k + 1) * 128, :]).then_inc(sem("w1l"), 16)
            for t in range(8, 16):
                gp.wait_ge(sem("tcast"), 2 * (t + 1))
                gp.dma_start(out=w2q_hbm[:, (t - 8) * 4096:(t - 7) * 4096], in_=bfv[:, :]).then_inc(sem("w2st"), 16)
            bar(gp, 2)
            for i in range(NCH):
                gp.wait_ge(sem("lms"), 16 * (i + 1))
                gp.collective_compute(
                    "AllGather", ALU.bypass, RG,
                    ins=[agl_in[i:i + 1, 0:1]],
                    outs=[lossag[i:i + 1, 0:8]],
                ).then_inc(sem("agd"), 1)
            gp.wait_ge(sem("mail"), 16 * 64)
            gp.collective_compute(
                "AllGather", ALU.bypass, RG,
                ins=[agy_in[0:33, :].rearrange("(o a) b -> o (a b)", o=1)],
                outs=[hal_all[0:528, :].rearrange("(o a) b -> o (a b)", o=1)],
            ).then_inc(sem("agd"), 1)
            bar(gp, 3)
            gp.wait_ge(sem("agd"), NCH + 1)
            gp.wait_ge(sem("z0"), 16)
            gp.wait_ge(sem("ld0"), 16 * 4)
            for c in range(NCH):
                if c >= 2:
                    gp.wait_ge(sem("q2"), c - 1)
                gp.indirect_dma_start(
                    out=yt[c % 2][0:2, :], out_offset=None,
                    in_=hal_all, in_offset=IndirectOffsetOnAxis(sidx[0:2, c:c + 1], 0),
                ).then_inc(sem("hg"), 16)
            bar(gp, 4)

        @block.tensor
        def _(tn):
            tn.wait_ge(sem("ldw"), 16 * 24)
            n = 0
            for g in range(4):
                for uvi in range(2):
                    wsrc = w0bf if uvi == 0 else w2bf
                    for m in range(32):
                        if n >= 2:
                            tn.wait_ge(sem("ev0"), n - 1)
                        bank = pA if n % 2 == 0 else pB
                        for k in range(8):
                            mm = tn.matmul(bank[:, :], wsrc[:, k * FH + m * 128:k * FH + (m + 1) * 128],
                                           xtbf[:, k * NTOK + g * 512:k * NTOK + (g + 1) * 512],
                                           start=(k == 0), stop=(k == 7))
                        mm.then_inc(sem("mm0"), 1)
                        n += 1
            tn.wait_ge(sem("mn"), NW + 1)
            for wi in range(2):
                mm = tn.matmul(pS[:, wi:wi + 1], ones128[:, :], accw[wi][0][:, 0:1], start=True, stop=True)
            mm.then_inc(sem("sw1"), 1)
            bar(tn, 1)
            bar(tn, 2)
            tn.wait_ge(sem("w1l"), 16 * 32)
            for i in range(NCH):
                tn.wait_ge(sem("gh"), 2 * i + 1)
                if i >= 1:
                    tn.wait_ge(sem("ge1"), i)
                for ng in range(2):
                    bank = pA if ng == 0 else pB
                    for k in range(32):
                        if ng == 0 and k == 16:
                            tn.wait_ge(sem("gh"), 2 * i + 2)
                        mm = tn.matmul(bank[0:64, :], gbf[:, k * 64:(k + 1) * 64],
                                       w1bf[:, k * 1024 + ng * 512:k * 1024 + (ng + 1) * 512],
                                       start=(k == 0), stop=(k == 31))
                    mm.then_inc(sem("mm1"), 1)
                tn.wait_ge(sem("lq"), i + 1)
                if i >= 1:
                    tn.wait_ge(sem("lse"), i)
                tn.matmul(pS[0:1, 0:1], ones_col[:, 0:1], lnsml[:, 0:1], start=True, stop=True).then_inc(sem("tm"), 1)
                tn.wait_ge(sem("lrq"), 16 * (i + 1))
                tn.wait_ge(sem("d"), i + 1)
                tn.matmul(pC[:, 0:8], E128[:, :], lrp[:, 0:8], start=True, stop=True).then_inc(sem("bc"), 1)
            bar(tn, 3)
            tn.wait_ge(sem("ld0"), 16 * 4)
            tn.wait_ge(sem("w2l"), 16 * 4)
            TPN = [0]

            def tpose(p, cols, in_ap, idn_ap):
                T = TPN[0]
                if T >= 2:
                    tn.wait_ge(sem("yqe"), T - 1)
                tn.transpose(TPB[T % 2][0:p, cols[0]:cols[1]], in_ap, idn_ap).then_inc(sem("tpq"), 1)
                TPN[0] += 1

            for b in range(8):
                for j in range(4):
                    c = 4 * b + j
                    tn.wait_ge(sem("q2"), c + 1)
                    for k in range(8):
                        tpose(128, (0, 66), yqb[:, k * 128:(k + 1) * 128], idn_b[0:66, 0:66])
                    tn.wait_ge(sem("q1"), c + 1)
                    tpose(1, (0, 66), aycb[0:66, 0:1], idn_b[0:66, 0:66])
                tn.wait_ge(sem("yqe"), 40 * b + 36)
                if b >= 1:
                    tn.wait_ge(sem("sb"), b)
                tn.matmul(pA[:, 0:264], E128[:, :], mvp_s[:, 0:264], start=True, stop=True).then_inc(sem("sbT"), 1)
                if b >= 1:
                    tn.wait_ge(sem("shc"), b)
                for m in range(32):
                    if m >= 2:
                        tn.wait_ge(sem("hT"), 32 * b + m - 1)
                    bank = pE if m % 2 == 0 else pF
                    for k in range(8):
                        mm = tn.matmul(bank[:, 0:264], w1qT[:, k * FH + m * 128:k * FH + (m + 1) * 128],
                                       yqT[:, k * 264:(k + 1) * 264], start=(k == 0), stop=(k == 7))
                    mm.then_inc(sem("up"), 1)
                tn.wait_ge(sem("rmb"), b + 1)
                tn.wait_ge(sem("hT"), 32 * (b + 1))
                tpose(128, (66, 194), rmaxb[:, 0:128], idn_b[:, :])
                tpose(128, (66, 194), rmaxb[:, 128:256], idn_b[:, :])
                tn.wait_ge(sem("shc"), b + 1)
                tpose(1, (194, 322), shb16[:, 0:1], idn_b[:, :])
                tpose(1, (194, 322), shb16[:, 1:2], idn_b[:, :])
                tn.wait_ge(sem("yqe"), 40 * (b + 1))
                if b >= 1:
                    tn.wait_ge(sem("amQ"), b)
                tn.matmul(pB[:, 0:256], E128[:, :], mvp_q[:, 0:256], start=True, stop=True).then_inc(sem("sqT"), 1)
                tn.wait_ge(sem("qa2"), 32 * (b + 1))
                for p2 in range(2):
                    gp_ = 2 * b + p2
                    if gp_ >= 1:
                        tn.wait_ge(sem("de1"), gp_)
                    for ng in range(2):
                        bank = pC if ng == 0 else pD
                        for k in range(32):
                            mm = tn.matmul(bank[:, :], hqT[:, k * 256 + p2 * 128:k * 256 + (p2 + 1) * 128],
                                           w2qT[:, k * 1024 + ng * 512:k * 1024 + (ng + 1) * 512],
                                           start=(k == 0), stop=(k == 31))
                    mm.then_inc(sem("dn"), 1)
            bar(tn, 4)

        @block.vector
        def _(ve):
            ve.memset(ones_col[:, :], 1.0)
            ve.memset(E128[:, :], 0.0)
            ve.memset(E128[0:1, :], 1.0)
            ve.memset(ones128[:, :], 1.0)
            ve.memset(lrp[:, :], 0.0)
            for wi in range(2):
                for pp in range(2):
                    ve.memset(accw[wi][pp][:, :], 0.0)
            for t in range(NW):
                wi, tl = (0, t) if t < 8 else (1, t - 8)
                if t % 2 == 0:
                    ve.wait_ge(sem("fw0E"), CUME[t])
                else:
                    ve.wait_ge(sem("fw0O"), CUMO[t])
                ve.tensor_reduce(t8a[:, 2:3], fwbuf[t % 2][:, :], AX.X, ALU.add, apply_absolute_value=True)
                gap(ve)
                ve.tensor_tensor(accw[wi][(tl + 1) % 2][:, 0:1], accw[wi][tl % 2][:, 0:1], t8a[:, 2:3], ALU.add).then_inc(sem("mn"), 2 if t == NW - 1 else 1)
            ve.wait_ge(sem("sw1"), 1)
            for wi in range(2):
                ve.tensor_scalar(t8b[:, 0:1], pS[:, wi:wi + 1], 1.0 / WCNT, None, ALU.mult)
                gap(ve)
                ve.tensor_scalar_max(mwc[wi][:, 0:1], t8b[:, 0:1], 1e-5)
                gap(ve)
                r = recip1(ve, swq[wi], mwc[wi][:, 0:1], t8a)
            r.then_inc(sem("swd"), 1)
            bar(ve, 1)
            for t in range(NW):
                if t % 2 == 0:
                    ve.wait_ge(sem("fwbE"), CUME[t])
                else:
                    ve.wait_ge(sem("fwbO"), CUMO[t])
                if t >= 1:
                    ve.wait_ge(sem("tcast"), 2 * t)
                if t >= 9:
                    ve.wait_ge(sem("w2st"), 16 * (t - 8))
                wi = 0 if t < 8 else 1
                ve.tensor_scalar(ftmp[:, :], fsb[t % 2][:, :], swq[wi][:, 0:1], -1.0, ALU.mult, ALU.max)
                ve.tensor_scalar(ftmp[:, :], ftmp[:, :], 1.0, None, ALU.min).then_inc(sem("tclip"), 1)
            bar(ve, 2)
            ve.memset(dbuf[0][:, :], 1.0)
            ve.memset(lnsml[:, :], 0.0)
            ve.memset(d3buf[0][:, :], 1.0).then_inc(sem("d"), 1)
            ve.wait_ge(sem("uvl"), 16)
            ve.wait_ge(sem("vvl"), 16)
            ve.tensor_tensor(uvp[0][:, :], ubuf[0][:, :], vbuf[0][:, :], ALU.mult).then_inc(sem("uvpd"), 1)
            for i in range(NCH):
                for hf in range(2):
                    ve.wait_ge(sem("sigh"), 2 * i + hf + 1)
                    ve.tensor_tensor(gbf[:, hf * 1024:(hf + 1) * 1024], tsig[:, :],
                                     uvp[i % 2][:, hf * 1024:(hf + 1) * 1024], ALU.mult).then_inc(sem("gh"), 1)
                if i + 1 < NCH:
                    ve.wait_ge(sem("uvl"), 16 * (i + 2))
                    ve.wait_ge(sem("vvl"), 16 * (i + 2))
                    ve.tensor_tensor(uvp[(i + 1) % 2][:, :], ubuf[(i + 1) % 3][:, :],
                                     vbuf[(i + 1) % 2][:, :], ALU.mult).then_inc(sem("uvpd"), 1)
                ve.wait_ge(sem("ge1"), i + 1)
                ve.wait_ge(sem("xvl"), 16 * (i + 1))
                ve.tensor_sub(tdf[:, :], g1t[:, :], xnb[i % 2][:, :]).then_inc(sem("df"), 1)
                ve.tensor_add(trs[:, :], g1t[:, :], xnb[i % 2][:, :]).then_inc(sem("rs"), 1)
                ve.tensor_reduce(lnsml[0:64, 2:3], trs[:, :], AX.X, ALU.add)
                gap(ve)
                ve.tensor_scalar(lnsml[0:64, 3:4], lnsml[0:64, 2:3], 1.0 / H, None, ALU.mult)
                gap(ve)
                ve.tensor_tensor(lnsml[0:64, 4:5], lnsml[0:64, 3:4], lnsml[0:64, 3:4], ALU.mult)
                ve.wait_ge(sem("ln1"), i + 1)
                ve.tensor_scalar(lnsml[0:64, 5:6], lnsml[0:64, 1:2], 1.0 / H, EPS, ALU.mult, ALU.add)
                gap(ve)
                ve.tensor_tensor(lnsml[0:64, 6:7], lnsml[0:64, 5:6], lnsml[0:64, 4:5], ALU.subtract).then_inc(sem("ln2"), 1)
                ve.wait_ge(sem("ln3"), i + 1)
                recip1(ve, t8b, lnsml[0:64, 7:8], t8a, p=64)
                gap(ve)
                if i >= 2:
                    ve.wait_ge(sem("mail"), 32 * (i - 1))
                    ve.wait_ge(sem("yst"), 16 * (i - 1))
                ve.tensor_scalar(ybuf[i % 2][:, :], trs[:, :], lnsml[0:64, 3:4], t8b[0:64, 0:1],
                                 ALU.subtract, ALU.mult).then_inc(sem("yy"), 1)
                ve.wait_ge(sem("tm"), i + 1)
                ve.tensor_copy(lsb[i % 2][0:1, 0:1], pS[0:1, 0:1]).then_inc(sem("lse"), 1)
                ve.wait_ge(sem("bc"), i + 1)
                ve.tensor_reduce(t8a[:, 0:1], pC[:, 0:8], AX.X, ALU.add)
                gap(ve)
                ve.tensor_scalar(t8b[:, 1:2], t8a[:, 0:1], -C_LOSS, 1.0, ALU.mult, ALU.add)
                gap(ve)
                ve.tensor_tensor(dbuf[(i + 1) % 2][:, 0:1], dbuf[i % 2][:, 0:1], t8b[:, 1:2], ALU.mult)
                gap(ve)
                ve.tensor_tensor(t8a[:, 1:2], dbuf[(i + 1) % 2][:, 0:1], dbuf[(i + 1) % 2][:, 0:1], ALU.mult)
                gap(ve)
                ve.tensor_tensor(d3buf[(i + 1) % 2][:, 0:1], t8a[:, 1:2], dbuf[(i + 1) % 2][:, 0:1], ALU.mult).then_inc(sem("d"), 1)
            bar(ve, 3)
            ve.memset(yAB[0][0:1, :], 0.0)
            ve.memset(mvp_s[:, :], 0.0)
            ve.memset(mvp_q[:, :], 0.0).then_inc(sem("z0v"), 1)
            ve.wait_ge(sem("ld0"), 16 * 4)
            ve.tensor_scalar(t8a[:, 3:4], mwc[0][:, 0:1], 1.0 / 127.0, None, ALU.mult)
            gap(ve)
            ve.tensor_scalar_mul(cw_s[:, :], cw_t[:, :], t8a[:, 3:4]).then_inc(sem("cws"), 1)
            TPN = [0]
            for b in range(8):
                ve.memset(rmax[:, :], 0.0)
                ve.memset(rmin[:, :], 0.0)
                if b >= 1:
                    ve.wait_ge(sem("up"), 32 * b)
                    ve.wait_ge(sem("sbT"), b)
                    ve.wait_ge(sem("sqT"), b)
                for j in range(4):
                    c = 4 * b + j
                    ve.wait_ge(sem("yl"), 16 * (c + 1))
                    ve.wait_ge(sem("hg"), 16 * (c + 1))
                    ve.tensor_reduce(ayt[0:66, 0:1], yt[c % 2][:, :], AX.X, ALU.max, apply_absolute_value=True)
                    gap(ve)
                    ve.tensor_scalar_max(ayt[0:66, 1:2], ayt[0:66, 0:1], 1e-5)
                    gap(ve)
                    ve.tensor_copy(aycb[0:66, 0:1], ayt[0:66, 1:2])
                    gap(ve)
                    ve.tensor_copy(ayf[0:66, 0:1], aycb[0:66, 0:1])
                    gap(ve)
                    ve.tensor_scalar_mul(t66a[0:66, 0:8], ones_col[0:66, 0:8], ayf[0:66, 0:1])
                    gap(ve)
                    ve.reciprocal(t66b[0:66, 0:8], t66a[0:66, 0:8])
                    gap(ve)
                    ve.tensor_scalar(sy66[0:66, 0:1], t66b[0:66, 0:1], 127.0, None, ALU.mult).then_inc(sem("q1"), 1)
                    for k in range(8):
                        T = TPN[0]
                        ve.wait_ge(sem("tpq"), T + 1)
                        dstv = yqT.rearrange("p (kk x) -> p kk x", kk=8)[:, k, j * 66:(j + 1) * 66]
                        ve.tensor_copy(dstv, TPB[T % 2][:, 0:66]).then_inc(sem("yqe"), 1)
                        TPN[0] += 1
                    T = TPN[0]
                    ve.wait_ge(sem("tpq"), T + 1)
                    ve.tensor_copy(mvp_s[0:1, j * 66:(j + 1) * 66], TPB[T % 2][0:1, 0:66]).then_inc(sem("yqe"), 1)
                    TPN[0] += 1
                ve.wait_ge(sem("sbT"), b + 1)
                ve.tensor_copy(sinv_bc[:, :], pA[:, 0:264]).then_inc(sem("sb"), 1)
                ve.wait_ge(sem("cws"), 1)
                for m in range(32):
                    ve.wait_ge(sem("up"), 32 * b + m + 1)
                    bank = pE if m % 2 == 0 else pF
                    ve.tensor_tensor(htmp[m % 2][:, :], bank[:, 0:264], sinv_bc[:, :], ALU.mult).then_inc(sem("hT"), 1)
                    ve.wait_ge(sem("cva"), 32 * b + m + 1)
                    wa = cw_s[:, 3 * m + 0:3 * m + 1]
                    wc = cw_s[:, 3 * m + 2:3 * m + 3]
                    h4 = htmp[m % 2].rearrange("p (g x) -> p g x", g=4)
                    A4 = cvA.rearrange("p (g x) -> p g x", g=4)
                    B4 = cvB.rearrange("p (g x) -> p g x", g=4)
                    C4 = cvC.rearrange("p (g x) -> p g x", g=4)
                    ve.scalar_tensor_tensor(B4[:, :, 2:3], h4[:, :, 0:1], wa, A4[:, :, 2:3], ALU.mult, ALU.add)
                    ve.scalar_tensor_tensor(B4[:, :, 3:66], h4[:, :, 2:65], wa, A4[:, :, 3:66], ALU.mult, ALU.add)
                    ve.scalar_tensor_tensor(C4[:, :, 2:65], h4[:, :, 3:66], wc, B4[:, :, 2:65], ALU.mult, ALU.add)
                    ve.scalar_tensor_tensor(C4[:, :, 65:66], h4[:, :, 1:2], wc, B4[:, :, 65:66], ALU.mult, ALU.add).then_inc(sem("cvd"), 1)
                    ve.wait_ge(sem("sil"), 32 * b + m + 1)
                    ve.tensor_tensor(rmax[:, :], rmax[:, :], hsb[:, m * 256:(m + 1) * 256], ALU.max)
                    ve.tensor_tensor(rmin[:, :], rmin[:, :], hsb[:, m * 256:(m + 1) * 256], ALU.min).then_inc(sem("cvv"), 1)
                ve.tensor_scalar(rmin[:, :], rmin[:, :], -1.0, None, ALU.mult)
                ve.tensor_tensor(rmax[:, :], rmax[:, :], rmin[:, :], ALU.max)
                ve.tensor_copy(rmaxb[:, :], rmax[:, :]).then_inc(sem("rmb"), 1)
                for q in range(2):
                    T = TPN[0]
                    ve.wait_ge(sem("tpq"), T + 1)
                    ve.tensor_reduce(amaxh[:, q:q + 1], TPB[T % 2][:, 66:194], AX.X, ALU.max).then_inc(sem("yqe"), 1)
                    TPN[0] += 1
                gap(ve)
                ve.tensor_scalar_max(amaxh[:, 2:4], amaxh[:, 0:2], 1e-5)
                gap(ve)
                for q in range(2):
                    recip1(ve, t8b, amaxh[:, 2 + q:3 + q], t8a)
                    gap(ve)
                    ve.tensor_scalar(t8a[:, 4:5], t8b[:, 0:1], 127.0, None, ALU.mult)
                    gap(ve)
                    ve.tensor_copy(shb16[:, q:q + 1], t8a[:, 4:5])
                    gap(ve)
                ve.tensor_copy(shbf[:, 0:2], shb16[:, 0:2])
                gap(ve)
                for q in range(2):
                    recip1(ve, t8b, shbf[:, q:q + 1], t8a)
                    gap(ve)
                    r = ve.tensor_scalar_mul(shv[:, q:q + 1], t8b[:, 0:1], mwc[1][:, 0:1])
                    gap(ve)
                r.then_inc(sem("shc"), 1)
                for q in range(2):
                    T = TPN[0]
                    ve.wait_ge(sem("tpq"), T + 1)
                    ve.tensor_copy(mvp_q[0:1, q * 128:(q + 1) * 128], TPB[T % 2][0:1, 194:322]).then_inc(sem("yqe"), 1)
                    TPN[0] += 1
                ve.wait_ge(sem("sqT"), b + 1)
                ve.tensor_copy(sq_bc[:, :], pB[:, 0:256]).then_inc(sem("amQ"), 1)
                for m in range(32):
                    if m >= 1:
                        ve.wait_ge(sem("qa"), 32 * b + m)
                    ve.tensor_tensor(qtmp, hsb[:, m * 256:(m + 1) * 256], sq_bc[:, :], ALU.mult).then_inc(sem("qv"), 1)
                for p2 in range(2):
                    gp_ = 2 * b + p2
                    ve.wait_ge(sem("de1"), gp_ + 1)
                    ve.wait_ge(sem("yab"), 16 * (gp_ + 1))
                    ve.tensor_add(outt[:, :], outt[:, :], yAB[gp_ % 2][:, :]).then_inc(sem("rv0"), 1)
                    ve.tensor_reduce(lnsml2[:, 2:3], outt[:, :], AX.X, ALU.add)
                    gap(ve)
                    ve.tensor_scalar(lnsml2[:, 3:4], lnsml2[:, 2:3], 1.0 / H, None, ALU.mult)
                    gap(ve)
                    ve.tensor_tensor(lnsml2[:, 4:5], lnsml2[:, 3:4], lnsml2[:, 3:4], ALU.mult)
                    ve.wait_ge(sem("ra1"), gp_ + 1)
                    ve.tensor_scalar(lnsml2[:, 5:6], lnsml2[:, 1:2], 1.0 / H, EPS, ALU.mult, ALU.add)
                    gap(ve)
                    ve.tensor_tensor(lnsml2[:, 6:7], lnsml2[:, 5:6], lnsml2[:, 4:5], ALU.subtract).then_inc(sem("rv2"), 1)
                    ve.wait_ge(sem("ra2"), gp_ + 1)
                    recip1(ve, t8b, lnsml2[:, 7:8], t8a)
                    gap(ve)
                    ve.tensor_scalar(yAB[gp_ % 2][:, :], outt[:, :], lnsml2[:, 3:4], t8b[:, 0:1],
                                     ALU.subtract, ALU.mult).then_inc(sem("ot"), 1)
            bar(ve, 4)

        @block.scalar
        def _(sc):
            for pr in fw_tiles(1, fwbuf[1]):
                sc.dma_start(out=pr[0], in_=pr[1]).then_inc(sem("fw0O"), 16)
            n = 0
            for g in range(4):
                for uvi in range(2):
                    for m in range(32):
                        if n % 16 == 0 and 2 + n // 16 < NW and (2 + n // 16) % 2 == 1:
                            t = 2 + n // 16
                            sc.wait_ge(sem("mn"), t - 1)
                            for pr in fw_tiles(t, fwbuf[t % 2]):
                                sc.dma_start(out=pr[0], in_=pr[1]).then_inc(sem("fw0O"), 16)
                        sc.wait_ge(sem("mm0"), n + 1)
                        if n >= 2:
                            if n % 2 == 0:
                                sc.wait_ge(sem("st0a"), 16 * (n // 2))
                            else:
                                sc.wait_ge(sem("st0b"), 16 * (n // 2))
                        bank = pA if n % 2 == 0 else pB
                        sc.copy(uvtmp[n % 2][:, :], bank[:, :]).then_inc(sem("ev0"), 1)
                        if n % 2 == 1:
                            src = uvtmp[1].rearrange("p (c t) -> p c t", c=8)
                            dst = uv_hbm[uvi, 8 * g:8 * g + 8, :, m * 64:(m + 1) * 64].rearrange("c p t -> p c t")
                            sc.dma_start(out=dst, in_=src).then_inc(sem("st0b"), 16)
                        n += 1
            bar(sc, 1)
            for t in range(NW):
                sc.wait_ge(sem("tclip"), t + 1)
                if t + 2 < NW and t % 2 == 0:
                    for pr in fw_tiles(t + 2, fsb[t % 2]):
                        sc.dma_start(out=pr[0], in_=pr[1]).then_inc(sem("fwbE"), 16)
                sc.copy(itmp[:, :], ftmp[:, :]).then_inc(sem("tcast"), 1)
                if t < 8:
                    sc.copy(w1qT[:, t * FH:(t + 1) * FH], itmp[:, :]).then_inc(sem("tcast"), 1)
                else:
                    if t > 8:
                        sc.wait_ge(sem("w2st"), 16 * (t - 8))
                    sc.copy(bfv[:, :], itmp[:, :]).then_inc(sem("tcast"), 1)
            bar(sc, 2)
            for j in range(2):
                sc.dma_start(out=vbuf[j][:, :], in_=uv_hbm[1, j, :, :]).then_inc(sem("vvl"), 16)
            for i in range(NCH):
                sc.wait_ge(sem("d"), i + 1)
                sc.wait_ge(sem("uvl"), 16 * (i + 1))
                for hf in range(2):
                    if 2 * i + hf >= 1:
                        sc.wait_ge(sem("gh"), 2 * i + hf)
                    sc.activation(tsig[:, :], ubuf[i % 3][:, hf * 1024:(hf + 1) * 1024], AF.Sigmoid,
                                  scale=dbuf[i % 2][:, 0:1]).then_inc(sem("sigh"), 1)
                d3ap = d3buf[i % 2][0:64, 0:1]
                sc.wait_ge(sem("mm1"), 2 * i + 1)
                sc.activation(g1t[:, 0:512], pA[0:64, :], AF.Identity, scale=d3ap)
                sc.wait_ge(sem("mm1"), 2 * i + 2)
                sc.activation(g1t[:, 512:1024], pB[0:64, :], AF.Identity, scale=d3ap).then_inc(sem("ge1"), 1)
                if i + 2 < NCH:
                    sc.dma_start(out=vbuf[(i + 2) % 2][:, :], in_=uv_hbm[1, i + 2, :, :]).then_inc(sem("vvl"), 16)
                sc.wait_ge(sem("df"), i + 1)
                sc.activation(gbf[0:64, 0:1024], tdf[:, :], AF.Square, accum_out=lnsml[0:64, 0:1]).then_inc(sem("lq"), 1)
                sc.wait_ge(sem("rs"), i + 1)
                sc.activation(gbf[0:64, 0:1024], trs[:, :], AF.Square, accum_out=lnsml[0:64, 1:2]).then_inc(sem("ln1"), 1)
                sc.wait_ge(sem("ln2"), i + 1)
                sc.activation(lnsml[0:64, 7:8], lnsml[0:64, 6:7], AF.Sqrt).then_inc(sem("ln3"), 1)
                sc.wait_ge(sem("lse"), i + 1)
                sc.dma_start(out=agl_in[i:i + 1, 0:1], in_=lsb[i % 2][0:1, 0:1]).then_inc(sem("lms"), 16)
            bar(sc, 3)
            for b in range(8):
                for j in range(4):
                    c = 4 * b + j
                    sc.wait_ge(sem("q1"), c + 1)
                    if c >= 1:
                        bb, jj = (c - 1) // 4, (c - 1) % 4
                        sc.wait_ge(sem("tpq"), 40 * bb + 9 * (jj + 1))
                    for hf in range(2):
                        sc.activation(yqi[:, :], yt[c % 2][:, hf * 512:(hf + 1) * 512], AF.Identity,
                                      scale=sy66[0:66, 0:1])
                        a2 = sc.activation(yqb[:, hf * 512:(hf + 1) * 512], yqi[:, :], AF.Identity)
                    a2.then_inc(sem("q2"), 1)
                for m in range(32):
                    sc.wait_ge(sem("hT"), 32 * b + m + 1)
                    if m >= 1:
                        sc.wait_ge(sem("cvd"), 32 * b + m)
                    sc.activation(cvA[:, :], htmp[m % 2][:, :], AF.Identity,
                                  scale=cw_s[:, 3 * m + 1:3 * m + 2], bias=cb_t[:, m:m + 1]).then_inc(sem("cva"), 1)
                    sc.wait_ge(sem("cvd"), 32 * b + m + 1)
                    c4 = cvC.rearrange("p (g x) -> p g x", g=4)[:, :, 2:66]
                    s4 = hsb[:, m * 256:(m + 1) * 256].rearrange("p (g x) -> p g x", g=4)
                    sc.activation(s4, c4, AF.Silu).then_inc(sem("sil"), 1)
                if b >= 1:
                    sc.wait_ge(sem("dn"), 2 * b)
                for m in range(32):
                    sc.wait_ge(sem("qv"), 32 * b + m + 1)
                    sc.copy(hint[:, :], qtmp).then_inc(sem("qa"), 1)
                    sc.copy(hqT[:, m * 256:(m + 1) * 256], hint[:, :]).then_inc(sem("qa2"), 1)
                for p2 in range(2):
                    gp_ = 2 * b + p2
                    sc.wait_ge(sem("dn"), gp_ + 1)
                    if gp_ >= 1:
                        sc.wait_ge(sem("ot"), gp_)
                    shp = shv[:, p2:p2 + 1]
                    sc.activation(outt[:, 0:512], pC[:, :], AF.Identity, scale=shp)
                    sc.activation(outt[:, 512:1024], pD[:, :], AF.Identity, scale=shp).then_inc(sem("de1"), 1)
                    sc.wait_ge(sem("rv0"), gp_ + 1)
                    sc.activation(yAB[gp_ % 2][:, :], outt[:, :], AF.Square, accum_out=lnsml2[:, 1:2]).then_inc(sem("ra1"), 1)
                    sc.wait_ge(sem("rv2"), gp_ + 1)
                    sc.activation(lnsml2[:, 7:8], lnsml2[:, 6:7], AF.Sqrt).then_inc(sem("ra2"), 1)
            bar(sc, 4)

    return nc


def kernel(L0_attn_x, L0_attn_w0, L0_attn_w1, L0_attn_w2,
           L0_ffn_w1, L0_ffn_conv_w, L0_ffn_conv_b, L0_ffn_w2):
    x = np.ascontiguousarray(np.asarray(L0_attn_x, np.float32)[0])
    w0 = np.ascontiguousarray(np.asarray(L0_attn_w0, np.float32))
    w1 = np.ascontiguousarray(np.asarray(L0_attn_w1, np.float32))
    w2 = np.ascontiguousarray(np.asarray(L0_attn_w2, np.float32))
    fw1 = np.asarray(L0_ffn_w1, np.float32)
    fw2 = np.asarray(L0_ffn_w2, np.float32)
    cw = np.asarray(L0_ffn_conv_w, np.float32)
    cb = np.asarray(L0_ffn_conv_b, np.float32)

    fw1T = np.ascontiguousarray(fw1.T)
    fw2T = np.ascontiguousarray(fw2.T)
    cwv = np.ascontiguousarray(cw[:, 0, :].reshape(32, 128, 3).transpose(1, 0, 2).reshape(128, 96))
    cbv = np.ascontiguousarray(cb.reshape(32, 128).T)
    idnb = np.eye(128, dtype=ml_dtypes.bfloat16)

    nc = build_nc()
    in_maps = []
    tok_rows = []
    for c in range(8):
        rows = (np.arange(NCH)[:, None] * CHT + 64 * c + np.arange(64)[None, :]).reshape(-1)
        tok_rows.append(rows)
        xn = np.ascontiguousarray(x[rows])
        xt = np.ascontiguousarray(xn.T)
        hidx = np.zeros((2, NCH), np.uint32)
        for j in range(NCH):
            if c > 0:
                hidx[0, j] = ((c - 1) * 33 + (j + 1)) * 2 + 1
            elif j > 0:
                hidx[0, j] = (7 * 33 + j) * 2 + 1
            else:
                hidx[0, j] = ZROW
            if c < 7:
                hidx[1, j] = ((c + 1) * 33 + (j + 1)) * 2 + 0
            elif j < NCH - 1:
                hidx[1, j] = (0 * 33 + (j + 2)) * 2 + 0
            else:
                hidx[1, j] = ZROW
        in_maps.append({
            "xn": xn, "xt": xt, "w0": w0, "w2": w2, "w1": w1,
            "fw1T": fw1T, "fw2T": fw2T, "cwv": cwv, "cbv": cbv,
            "idnb": idnb, "hidx": hidx,
        })

    res = run_bass_kernel_spmd(nc, in_maps, core_ids=list(range(8)))
    out = np.empty((S, H), np.float32)
    for c in range(8):
        out[tok_rows[c]] = res.results[c]["o"]
    return out[None]
